# version 19
# baseline (speedup 1.0000x reference)
import numpy as np
import ml_dtypes
from contextlib import ExitStack

import concourse.bass as bass
import concourse.mybir as mybir
from concourse.bass_utils import run_bass_kernel_spmd

F32 = mybir.dt.float32
BF16 = mybir.dt.bfloat16
AF = mybir.ActivationFunctionType
ALU = mybir.AluOpType
AX = mybir.AxisListType
BF_NP = ml_dtypes.bfloat16

NCORES = 8
DM = 2048
NFC = 16
H = 16
TCORE = 4096
TA = 1024
NGA = TCORE // TA
TB = 512
NGB = TCORE // TB
QKV = 6144
NN = 8192
NNC = NN // 128
EPS = np.float32(1e-5)
RDM = float(1.0 / DM)


class Sem:
    __slots__ = ("h", "name", "v")

    def __init__(self, h, name):
        self.h, self.name, self.v = h, name, 0

    def plan(self, amt):
        self.v += amt
        return self.v


class KB:
    ENGINES = ("sync", "gpsimd", "scalar", "vector", "tensor")

    def __init__(self):
        self.streams = {e: [] for e in self.ENGINES}
        self.lastw = {}

    def wait(self, stream, tok):
        if tok is None:
            return
        sem, val = tok
        if val is None or val <= 0:
            return
        key = (stream, sem.name)
        if self.lastw.get(key, 0) >= val:
            return
        self.lastw[key] = val
        self.streams[stream].append(("w", sem, val))

    def op(self, stream, fn, waits=(), inc=None, amt=1):
        for t in waits:
            self.wait(stream, t)
        if inc is not None:
            if amt == "default":
                tok = inc.plan(1)
                self.streams[stream].append(("d", fn, inc))
            else:
                tok = inc.plan(amt)
                self.streams[stream].append(("i", fn, inc, amt))
            return (inc, tok)
        self.streams[stream].append(("f", fn))
        return None

    def replay(self, stream, eng):
        for item in self.streams[stream]:
            if item[0] == "w":
                eng.wait_ge(item[1].h, item[2])
            elif item[0] == "i":
                item[1](eng).then_inc(item[2].h, item[3])
            elif item[0] == "d":
                item[1](eng).then_inc(item[2].h)
            else:
                item[1](eng)


def build_nc(dbg=None, ngA=NGA, ngB=NGB, no_cc=False, a_stop=None, b_stop=None):
    nc = bass.Bass("TRN2", target_bir_lowering=False, debug=False)

    x_t = nc.dram_tensor("x_t", [DM, TCORE], F32, kind="ExternalInput").ap()
    w_qkv = nc.dram_tensor("w_qkv", [DM, QKV], F32, kind="ExternalInput").ap()
    w_out = nc.dram_tensor("w_out", [DM, DM], F32, kind="ExternalInput").ap()
    w0 = nc.dram_tensor("w0", [NN, DM], F32, kind="ExternalInput").ap()
    w1 = nc.dram_tensor("w1", [DM, NN], F32, kind="ExternalInput").ap()
    d_in = nc.dram_tensor("d_in", [1, H], F32, kind="ExternalInput").ap()
    b0r_in = nc.dram_tensor("b0r_in", [128, NNC], F32, kind="ExternalInput").ap()
    b1r_in = nc.dram_tensor("b1r_in", [128, NFC], F32, kind="ExternalInput").ap()
    sel_in = nc.dram_tensor("sel_in", [H, H * 128], BF16, kind="ExternalInput").ap()
    eones_in = nc.dram_tensor("eones_in", [128, H * 16], BF16, kind="ExternalInput").ap()
    iota_in = nc.dram_tensor("iota_in", [16, TB], F32, kind="ExternalInput").ap()
    iota128_in = nc.dram_tensor("iota128_in", [128, TB], F32, kind="ExternalInput").ap()
    ecnt_in = nc.dram_tensor("ecnt_in", [128, 256], F32, kind="ExternalInput").ap()
    maskf_in = nc.dram_tensor("maskf_in", [128, 256], F32, kind="ExternalInput").ap()
    onehot_in = nc.dram_tensor("onehot_in", [128, 8], F32, kind="ExternalInput").ap()

    y_t = nc.dram_tensor("y_t", [DM, TCORE], F32, kind="ExternalOutput").ap()

    wout_bf = nc.dram_tensor("wout_bf", [DM, DM], BF16, kind="Internal").ap()
    wqkv_bf = nc.dram_tensor("wqkv_bf", [DM, QKV], BF16, kind="Internal").ap()
    w0_bf = nc.dram_tensor("w0_bf", [NN, DM], BF16, kind="Internal").ap()
    w1_bf = nc.dram_tensor("w1_bf", [DM, NN], BF16, kind="Internal").ap()
    w0t_s = nc.dram_tensor("w0t_s", [NFC * 128, NN], BF16, kind="Internal").ap()
    w1t_s = nc.dram_tensor("w1t_s", [NNC * 128, DM], BF16, kind="Internal").ap()
    q_s = nc.dram_tensor("q_s", [128, H * TCORE], BF16, kind="Internal").ap()
    kv_s = nc.dram_tensor("kv_s", [128, H * TCORE], BF16, kind="Internal").ap()
    ks_s = nc.dram_tensor("ks_s", [128, H * TCORE], BF16, kind="Internal").ap()
    lnd_d = nc.dram_tensor("lnd_d", [H, 1], F32, kind="Internal").ap()
    cc_in = nc.dram_tensor("cc_in", [128, 256], F32, kind="Internal")
    cc_out = nc.dram_tensor("cc_out", [128, 256], F32, kind="Internal")

    kb = KB()
    es = ExitStack()

    def sb(name, shape, dtype):
        return es.enter_context(nc.sbuf_tensor(name, shape, dtype))

    def mksem(name):
        return Sem(es.enter_context(nc.semaphore(name)), name)

    ABF = sb("ABF", [128, 64512], BF16)
    AFP = sb("AFP", [128, 20096], F32)

    def abf(off, n, p0=0, p1=128):
        return ABF[p0:p1, off:off + n]

    def afp(off, n, p0=0, p1=128):
        return AFP[p0:p1, off:off + n]

    A_XBF, A_KG, A_VG, A_WQP, A_ROT = 0, 16384, 32768, 49152, 57344
    A_XSQ = 59392
    AB_OFF = 62464
    assert AB_OFF + 2048 <= 64512
    B_QG, B_KVG, B_KSG, B_ATT = 0, 8192, 16384, 24576
    B_DGT = 0
    B_WST = 32768
    B_XSQ = B_WST + 8192
    B_X3N = 49152
    B_EBF = 49152
    B_W1P = 57344
    B_ROT = 59392
    B_CSL = 61440

    FA_SCANO, FA_U, FA_KF, FA_RN = 0, 2048, 3072, 4096
    FB_R = 0
    S_MU, S_M2, S_VAR, S_A, S_B = 8192, 9216, 10240, 11264, 12288
    S_IOTA, S_DPW, S_DEN, S_RECD = 13312, 13824, 14336, 14848
    S_ECNT, S_MASK, S_WZ, S_GATH, S_SLOT, S_CONTR = 15360, 15616, 15872, 16128, 16384, 16640
    S_SEL, S_EONES = 16896, 17920
    S_ARB, S_BRB = 18048, 18560
    S_DPWB, S_RECDB = 19072, 19328
    S_IOTA128 = 19584
    assert S_IOTA128 + 512 <= 20096

    mu_r = afp(S_MU, TA, 0, 1)
    m2_r = afp(S_M2, TA, 0, 1)
    var_r = afp(S_VAR, TA, 0, 1)
    a_r = afp(S_A, TA, 0, 1)
    b_r = afp(S_B, TA, 0, 1)
    iota16 = afp(S_IOTA, TB, 0, 16)
    dpw = afp(S_DPW, TB, 0, 16)
    den_f = afp(S_DEN, TB, 0, 16)
    recden = afp(S_RECD, TB, 0, 16)
    ecnt_s = afp(S_ECNT, 256)
    maskf_s = afp(S_MASK, 256)
    wzf = afp(S_WZ, 256)
    gath = afp(S_GATH, 256)
    slotm = afp(S_SLOT, 256)
    contrib = afp(S_CONTR, 256)
    sel_s = afp(S_SEL, 1024, 0, 16).bitcast(BF16)
    eones_s = afp(S_EONES, 128).bitcast(BF16)
    a_rb = afp(S_ARB, 512, 0, 1).bitcast(BF16)
    b_rb = afp(S_BRB, 512, 0, 1).bitcast(BF16)
    dpw_bf = afp(S_DPWB, 256, 0, 16).bitcast(BF16)
    iota128 = afp(S_IOTA128, TB)
    recden_bf = afp(S_RECDB, 256, 0, 16).bitcast(BF16)

    dec_pc = sb("dec_pc", [128, H], F32)
    lndbc = sb("lndbc", [128, H], F32)
    skv_st = sb("skv_st", [128, H], F32)
    sks_st = sb("sks_st", [128, H], F32)
    s_in = sb("s_in", [128, 32], F32)
    onehot_s = sb("onehot_s", [128, 8], F32)
    b0r = sb("b0r", [128, NNC], F32)
    b1r = sb("b1r", [128, NFC], F32)
    drow = sb("drow", [1, H], F32)
    lnd_row = sb("lnd_row", [1, H], F32)
    lnd16 = sb("lnd16", [H, 1], F32)
    ones1b = sb("ones1b", [1, 128], BF16)
    onescb = sb("onescb", [128, 1], BF16)
    ones1f = sb("ones1f", [1, 128], F32)
    PSn = [es.enter_context(nc.psum_tensor(f"ps{i}", [128, 512], F32)) for i in range(8)]

    s_con = mksem("s_con")
    s_wqp = [mksem("s_wqp0"), mksem("s_wqp1")]
    s_wst = [mksem("s_wst0"), mksem("s_wst1")]
    s_w1p = [mksem("s_w1p0"), mksem("s_w1p1")]
    s_bR = mksem("s_bR")
    qsem = [mksem(f"qsem{i}") for i in range(4)]
    tp_sem = [mksem(f"s_tp{b}") for b in range(6)]
    g_cwq4 = [mksem(f"g_cwq{i}") for i in range(4)]
    g_cw = mksem("g_cw")
    g_c0 = mksem("g_c0")
    g_c1 = mksem("g_c1")
    g_x = mksem("g_x")
    g_q = [mksem("g_q0"), mksem("g_q1")]
    g_kv = mksem("g_kv")
    g_ks = mksem("g_ks")
    g_cc = mksem("g_cc")
    g_mi = mksem("g_mi")
    g_dbg = mksem("g_dbg")
    vch = mksem("vch")
    sch = mksem("sch")
    gch = mksem("gch")
    pe = mksem("pe")
    ccs = mksem("ccs")
    sto = mksem("sto")
    dma_sems = [s_con, s_wqp[0], s_wqp[1], s_wst[0], s_wst[1], s_w1p[0], s_w1p[1],
                s_bR, qsem[0], qsem[1], qsem[2], qsem[3],
                tp_sem[0], tp_sem[1], tp_sem[2],
                tp_sem[3], tp_sem[4], tp_sem[5],
                g_cwq4[0], g_cwq4[1], g_cwq4[2], g_cwq4[3], g_cw,
                g_c0, g_c1, g_x, g_q[0], g_q[1], g_kv, g_ks, g_cc, g_mi, g_dbg, sto]

    def v_op(fn, waits=()):
        return kb.op("vector", fn, list(waits) + [(vch, vch.v)], inc=vch)

    def s_op(fn, waits=()):
        return kb.op("scalar", fn, list(waits) + [(sch, sch.v)], inc=sch)

    def g_op(fn, waits=()):
        return kb.op("gpsimd", fn, list(waits) + [(gch, gch.v)], inc=gch)

    def t_mm(fn, waits=(), mark=False):
        return kb.op("tensor", fn, waits, inc=pe if mark else None)

    def dma_sy(sem, dst, src, waits=(), **kw):
        return kb.op("sync", lambda e, d=dst, s=src, kw=kw: e.dma_start(d, s, **kw),
                     waits, inc=sem, amt=16)

    def dma_out(dst, src, waits=()):
        return kb.op("gpsimd", lambda e, d=dst, s=src: e.dma_start(d, s),
                     waits, inc=sto, amt=16)

    def dma_gp(sem, dst, src, waits=(), **kw):
        return kb.op("gpsimd", lambda e, d=dst, s=src, kw=kw: e.dma_start(d, s, **kw),
                     waits, inc=sem, amt=16)

    class PBank:
        __slots__ = ("i", "wr", "rd")

        def __init__(self, i):
            self.i, self.wr, self.rd = i, None, []

        @property
        def t(self):
            return PSn[self.i]

    banks = [PBank(i) for i in range(8)]

    def mm_group(bank, mms, waits=(), mm_waits=None):
        w = list(waits) + bank.rd
        tok = None
        n = len(mms)
        for j, fn in enumerate(mms):
            wj = list(w) if j == 0 else []
            if mm_waits is not None and mm_waits[j] is not None:
                wj = wj + [mm_waits[j]]
            tok = t_mm(fn, wj, mark=(j == n - 1))
        bank.wr = tok
        bank.rd = []
        return tok

    def bank_read(bank, emit, kind="v", waits=()):
        w = list(waits) + ([bank.wr] if bank.wr else [])
        tok = v_op(emit, w) if kind == "v" else s_op(emit, w)
        bank.rd.append(tok)
        return tok

    dbg_toks = []

    def dbg_dump(ap, r0=0, c0=0, wait=()):
        dst = y_t[r0:r0 + ap.shape[0], c0:c0 + ap.shape[-1]]
        if ap.dtype == F32:
            dbg_toks.append(dma_sy(g_dbg, dst, ap, waits=list(wait)))
        else:
            dbg_toks.append(dma_gp(g_dbg, dst, ap, waits=list(wait)))

    tk_ones = g_op(lambda e: e.memset(ones1b[:], 1.0))
    tk_onescb = g_op(lambda e: e.memset(onescb[:], 1.0))
    tk_o1f = g_op(lambda e: e.memset(ones1f[:], 1.0))
    g_op(lambda e: e.memset(skv_st[:], 0.0))
    tk_st0 = g_op(lambda e: e.memset(sks_st[:], 0.0))

    dma_sy(s_con, drow[:], d_in)
    dma_sy(s_con, b0r[:], b0r_in)
    dma_sy(s_con, b1r[:], b1r_in)
    dma_sy(s_con, sel_s, sel_in)
    dma_sy(s_con, eones_s, eones_in)
    dma_sy(s_con, iota16, iota_in)
    dma_sy(s_con, iota128, iota128_in)
    dma_sy(s_con, ecnt_s, ecnt_in)
    dma_sy(s_con, maskf_s, maskf_in)
    dma_sy(s_con, onehot_s[:], onehot_in)
    tk_consts = (s_con, s_con.v)
    tk_d = tk_consts


    tk_sg = s_op(lambda e: e.activation(drow[:], drow[:], AF.Sigmoid), [tk_d])
    mm_group(banks[0], [lambda e: e.matmul(PSn[0][0:128, 0:H], ones1f[:], drow[:],
                                           start=True, stop=True)], [tk_sg, tk_o1f])
    tk_decpc = bank_read(banks[0], lambda e: e.tensor_copy(dec_pc[:], PSn[0][0:128, 0:H]))
    tk_lnr = s_op(lambda e: e.activation(lnd_row[:], drow[:], AF.Ln), [tk_sg])
    tk_lnc = v_op(lambda e: e.tensor_scalar_max(lnd_row[:], lnd_row[:], -87.0), [tk_lnr])
    mm_group(banks[1], [lambda e: e.matmul(PSn[1][0:128, 0:H], ones1f[:], lnd_row[:],
                                           start=True, stop=True)], [tk_lnc, tk_o1f])
    tk_lndbc = bank_read(banks[1], lambda e: e.tensor_copy(lndbc[:], PSn[1][0:128, 0:H]))
    tk_l2d = dma_gp(g_mi, lnd_d.rearrange("h one -> one h"), lnd_row[:], [tk_lnc])
    tk_l16 = dma_gp(g_mi, lnd16[:], lnd_d, [tk_l2d], allow_slow_non_contiguous=True)

    def ln_make(src_chunk, T, deps, xsq_off, src_is_f32=False, deps_fc=None,
                csl_off=None):
        nts = T // 512
        sum_bk = [banks[2], banks[3]]
        sq_bk = [banks[4], banks[5]]
        slot_tok = [None, None]
        for fc in range(NFC):
            src = src_chunk(fc)
            slot = abf(xsq_off + (fc % 2) * T, T)
            w = list(deps) + ([slot_tok[fc % 2]] if slot_tok[fc % 2] else [])
            if deps_fc is not None and deps_fc[fc] is not None:
                w = w + [deps_fc[fc]]
            tok_sq = s_op(lambda e, slot=slot, src=src: e.activation(slot, src, AF.Square), w)
            if src_is_f32:
                co = csl_off if csl_off is not None else (xsq_off + 2 * T)
                cslot = abf(co + (fc % 2) * T, T)
                tok_cp = s_op(lambda e, cslot=cslot, src=src: e.copy(cslot, src))
                sum_src = cslot
            else:
                tok_cp = None
                sum_src = src
            tok_q = None
            for ts in range(nts):
                bs, bq = sum_bk[ts], sq_bk[ts]
                wsum = (list(deps) + bs.rd + [tk_onescb]) if fc == 0 else []
                if deps_fc is not None and deps_fc[fc] is not None:
                    wsum = wsum + [deps_fc[fc]]
                if tok_cp:
                    wsum = wsum + [tok_cp]
                t_mm(lambda e, bs=bs, s2=sum_src, o=ts * 512, fc=fc: e.matmul(
                    bs.t[0:1, :], onescb[:], s2[:, o:o + 512],
                    start=(fc == 0), stop=(fc == NFC - 1), skip_group_check=True),
                    wsum, mark=(fc == NFC - 1))
                tok_s = (pe, pe.v) if fc == NFC - 1 else None
                if tok_s:
                    bs.wr, bs.rd = tok_s, []
                wq = [tok_sq] + (bq.rd if fc == 0 else [])
                t_mm(lambda e, bq=bq, sl=slot, o=ts * 512, fc=fc: e.matmul(
                    bq.t[0:1, :], onescb[:], sl[:, o:o + 512],
                    start=(fc == 0), stop=(fc == NFC - 1), skip_group_check=True),
                    wq, mark=True)
                tok_q = (pe, pe.v)
                if fc == NFC - 1:
                    bq.wr, bq.rd = tok_q, []
            slot_tok[fc % 2] = tok_q
        for ts in range(nts):
            o = ts * 512
            bank_read(sum_bk[ts], lambda e, o=o, b=sum_bk[ts]: e.tensor_scalar_mul(
                mu_r[:, o:o + 512], b.t[0:1, :], RDM))
            bank_read(sq_bk[ts], lambda e, o=o, b=sq_bk[ts]: e.tensor_scalar_mul(
                m2_r[:, o:o + 512], b.t[0:1, :], RDM))
        v_op(lambda e: e.tensor_mul(var_r[:, 0:T], mu_r[:, 0:T], mu_r[:, 0:T]))
        tok_v = v_op(lambda e: e.tensor_sub(var_r[:, 0:T], m2_r[:, 0:T], var_r[:, 0:T]))
        s_op(lambda e: e.activation(var_r[:, 0:T], var_r[:, 0:T], AF.Ln, bias=EPS), [tok_v])
        tk_a = s_op(lambda e: e.activation(a_r[:, 0:T], var_r[:, 0:T], AF.Exp, scale=-0.5))
        v_op(lambda e: e.scalar_tensor_tensor(b_r[:, 0:T], mu_r[:, 0:T], -1.0,
                                              a_r[:, 0:T], ALU.mult, ALU.mult), [tk_a])
        v_op(lambda e: e.tensor_copy(a_rb[:, 0:T], a_r[:, 0:T]))
        tok_rows = v_op(lambda e: e.tensor_copy(b_rb[:, 0:T], b_r[:, 0:T]))
        return tok_rows

    def ln_bcast(T, tok_rows):
        a_bc = abf(AB_OFF, T)
        b_bc = abf(AB_OFF + T, T)
        tok = None
        for which, row, dst in ((0, a_rb, a_bc), (1, b_rb, b_bc)):
            for ts in range(T // 512):
                bk = banks[6 + (which + ts) % 2]
                mm_group(bk, [lambda e, bk=bk, row=row, o=ts * 512: e.matmul(
                    bk.t[0:128, 0:512], ones1b[:], row[:, o:o + 512],
                    start=True, stop=True)], [tok_rows, tk_ones])
                tok = bank_read(bk, lambda e, bk=bk, dst=dst, o=ts * 512:
                                e.tensor_copy(dst[:, o:o + 512], bk.t[0:128, 0:512]))
        return a_bc, b_bc, tok

    X = abf(A_XBF, 16384)
    KG = abf(A_KG, 16384)
    VG = abf(A_VG, 16384)
    WQPs = [abf(A_WQP, 4096), abf(A_WQP + 4096, 4096)]
    TPS = [afp(5120 + i * 512, 512).bitcast(BF16) for i in range(6)]
    EB = [abf(A_ROT + i * 512, 512) for i in range(2)]
    RB = [abf(A_ROT + 1024 + i * 512, 512) for i in range(2)]
    scano = [afp(FA_SCANO + i * 1024, 1024) for i in range(2)]
    u_f = afp(FA_U, 1024)
    kf_f = afp(FA_KF, 1024)
    rn_f = [afp(FA_RN + i * 512, 512) for i in range(2)]

    x3d = x_t.rearrange("(c p) t -> p c t", p=128)
    tk_X = None
    tok_qkv_last = None
    wqp_pe = [None, None]
    piece_tok = [None, None]
    cast_toks = {"wout": [], "w0": [], "w1": []}
    wq_cast = {}
    wq_prev = [None, None, None, None]
    tok_scan_all = None
    tok_eb = [None, None]
    tok_rb = [None, None]
    tok_rn = [None, None]
    tok_scano = [None, None]
    tok_sdma = [None, None]
    tok_u = None
    tok_kf = None
    tok_wqp_pe = None

    tp_pairs = []
    tp_state = [0]
    tp_pend = []
    tps_store = [None] * 6
    tp_nbuf = [0]

    def tp_add_w0(nqs, deps):
        for nq in nqs:
            for h in range(2):
                dep = deps[(nq, h)]
                r0 = nq * 2048 + h * 1024
                for fc in range(NFC):
                    tp_pairs.append((
                        w0_bf[r0: r0 + 1024, fc * 128: fc * 128 + 128],
                        w0t_s[fc * 128: fc * 128 + 128, r0: r0 + 1024],
                        dep))

    def tp_add_w1(dep):
        for ncg in range(NNC):
            for h in range(2):
                tp_pairs.append((
                    w1_bf[h * 1024: h * 1024 + 1024, ncg * 128: ncg * 128 + 128],
                    w1t_s[ncg * 128: ncg * 128 + 128, h * 1024: h * 1024 + 1024],
                    dep))

    def dma_sc(sem, dst, src, waits=(), **kw):
        return kb.op("scalar", lambda e, d=dst, s=src, kw=kw: e.dma_start(d, s, **kw),
                     waits, inc=sem, amt=16)

    def tp_flush():
        for bufslot, ttok, dst in tp_pend:
            tps_store[bufslot] = dma_gp(tp_sem[bufslot], dst, TPS[bufslot], [ttok])
        del tp_pend[:]

    def tp_emit(n):
        tp_flush()
        while n > 0 and tp_state[0] < len(tp_pairs):
            src, dst, dep = tp_pairs[tp_state[0]]
            bufslot = tp_nbuf[0] % 6
            tp_nbuf[0] += 1
            w = [t for t in (dep, tps_store[bufslot]) if t]
            ttok = dma_sy(tp_sem[bufslot], TPS[bufslot], src, w, transpose=True)
            tp_pend.append((bufslot, ttok, dst))
            tp_state[0] += 1
            n -= 1

    for g in range(ngA):
        ga = g * TA
        if ngA == 4 and a_stop is None:
            if g == 1:
                tp_add_w0([0, 1], {(nq, h): cast_toks["w0"][3 * (2 * nq + h + 1) - 1]
                                   for nq in (0, 1) for h in (0, 1)})
            elif g == 2:
                tp_add_w0([2, 3], {(nq, h): cast_toks["w0"][3 * (2 * nq + h + 1) - 1]
                                   for nq in (2, 3) for h in (0, 1)})
                tp_add_w1((g_c1, g_c1.v))
        x3v = X.rearrange("p (c t) -> p c t", t=TA)
        xq_tok = []
        for xq in range(4):
            t = dma_gp(g_x, x3v[:, 4 * xq: 4 * xq + 4, :],
                       x3d[:, 4 * xq: 4 * xq + 4, ga:ga + TA],
                       waits=([t for t in (tok_qkv_last, tk_X) if t] if xq == 0 else []))
            xq_tok.append(t)
        tk_X = xq_tok[3]

        tok_rows = ln_make(lambda fc: X[:, fc * TA: fc * TA + TA], TA,
                           [tk_consts], A_XSQ,
                           deps_fc=[xq_tok[fc // 4] for fc in range(NFC)])
        a_bc, b_bc, tok_bc = ln_bcast(TA, tok_rows)
        stat_toks = [b.wr for b in (banks[2], banks[3], banks[4], banks[5]) if b.wr]
        apply_toks = []
        for fc in range(NFC):
            sl = X[:, fc * TA: fc * TA + TA]
            op = g_op if fc % 2 == 0 else v_op
            op(lambda e, sl=sl, a=a_bc: e.tensor_mul(sl, sl, a), [tok_bc] + stat_toks)
            apply_toks.append(op(lambda e, sl=sl, b=b_bc: e.tensor_add(sl, sl, b)))
        tok_apply = apply_toks[-1]

        if dbg == "xn" and g == 0:
            for fc in range(NFC):
                dbg_dump(X[:, fc * TA: fc * TA + TA], fc * 128, 0, [tok_apply])
        if dbg == "ln1rows" and g == 0:
            dbg_dump(mu_r[:, 0:TA], 0, 0, [tok_rows])
            dbg_dump(m2_r[:, 0:TA], 1, 0, [tok_rows])
            dbg_dump(a_r[:, 0:TA], 2, 0, [tok_rows])
            dbg_dump(b_r[:, 0:TA], 3, 0, [tok_rows])
        if a_stop == "ln1":
            continue

        n_oblk = 24 if a_stop not in ("qkv1", "qkv2", "qkv6") else 2 * int(a_stop[3])
        oblk_order = []
        for i in range(8):
            oblk_order += [8 + i, 16 + i]
        oblk_order += list(range(8))
        k_tok = {}
        v_tok = {}

        def emit_wq_cast(idx2):
            if g == 0 and idx2 < 24:
                ob = oblk_order[idx2]
                par = idx2 % 4
                wq_cast[ob] = dma_gp(
                    g_cwq4[par], wqkv_bf[:, ob * 256: ob * 256 + 256],
                    w_qkv[:, ob * 256: ob * 256 + 256],
                    [wq_prev[par]] if wq_prev[par] else [])
                wq_prev[par] = wq_cast[ob]
        if g == 0:
            for i2 in range(4):
                emit_wq_cast(i2)

        def emit_scans(hpair):
            nonlocal tok_u, tok_kf
            for h in (2 * hpair, 2 * hpair + 1):
                tok_u = v_op(lambda e, h=h: e.tensor_mul(
                    u_f, KG[:, h * TA: h * TA + TA], VG[:, h * TA: h * TA + TA]),
                    [k_tok[h], v_tok[h]] + ([tok_u] if tok_u else []))
                tk_s1 = v_op(lambda e, h=h, d=scano[0]: e.tensor_tensor_scan(
                    d, dec_pc[:, h:h + 1].broadcast_to([128, TA]), u_f,
                    skv_st[:, h:h + 1], ALU.mult, ALU.add),
                    [tok_u, tk_decpc] + [t for t in (tok_sdma[0],) if t])
                v_op(lambda e, h=h, d=scano[0]: e.tensor_copy(
                    skv_st[:, h:h + 1], d[:, TA - 1:TA]), [tk_s1])
                tok_sdma[0] = dma_gp(g_kv, kv_s[:, h * TCORE + ga: h * TCORE + ga + TA],
                                     scano[0], [(vch, vch.v)])
                tok_kf = v_op(lambda e, h=h: e.tensor_copy(
                    kf_f, KG[:, h * TA: h * TA + TA]),
                    [k_tok[h]] + ([tok_kf] if tok_kf else []))
                tk_s2 = v_op(lambda e, h=h, d=scano[1]: e.tensor_tensor_scan(
                    d, dec_pc[:, h:h + 1].broadcast_to([128, TA]), kf_f,
                    sks_st[:, h:h + 1], ALU.mult, ALU.add),
                    [tok_kf, tk_st0] + [t for t in (tok_sdma[1],) if t])
                v_op(lambda e, h=h, d=scano[1]: e.tensor_copy(
                    sks_st[:, h:h + 1], d[:, TA - 1:TA]), [tk_s2])
                tok_sdma[1] = dma_gp(g_ks, ks_s[:, h * TCORE + ga: h * TCORE + ga + TA],
                                     scano[1], [(vch, vch.v)])
                tok_u = tk_s1
                tok_kf = tk_s2

        for idx in range(24):
            if idx >= n_oblk:
                break
            oblk = oblk_order[idx]
            psl = idx % 2
            wqp_w = [t for t in (wqp_pe[psl], piece_tok[psl]) if t]
            if oblk in wq_cast:
                wqp_w.append(wq_cast[oblk])
            tk_piece = dma_sy(
                s_wqp[psl],
                WQPs[psl].rearrange("p (c t) -> p c t", t=256),
                wqkv_bf.rearrange("(c p) n -> p c n", p=128)[:, :, oblk * 256:oblk * 256 + 256],
                waits=wqp_w)
            piece_tok[psl] = tk_piece
            emit_wq_cast(idx + 4)
            if g >= 1:
                tp_emit(4)
            for oc in range(2):
                och = oblk * 2 + oc
                for ts in range(2):
                    bk = banks[(och * 2 + ts) % 2]
                    mms = [(lambda e, bk=bk, fc=fc, oc=oc, ts=ts, psl=psl: e.matmul(
                        bk.t[:, :],
                        WQPs[psl][:, fc * 256 + oc * 128: fc * 256 + oc * 128 + 128],
                        X[:, fc * TA + ts * 512: fc * TA + ts * 512 + 512],
                        start=(fc == 0), stop=(fc == NFC - 1)))
                        for fc in range(NFC)]
                    tok_qkv_last = mm_group(bk, mms, [tk_piece],
                                            mm_waits=apply_toks)
                    wqp_pe[psl] = tok_qkv_last
                    co = ts * 512
                    rs = (och * 2 + ts) % 2
                    if och < 16:
                        h = och
                        t1 = bank_read(bk, lambda e, bk=bk, r=rn_f[rs]: e.activation(
                            r, bk.t[:, :], AF.Relu, scale=-1.0), "s",
                            [tok_rn[rs]] if tok_rn[rs] else [])
                        t2 = s_op(lambda e, r=rn_f[rs], d=EB[rs]: e.activation(
                            d, r, AF.Exp, scale=-1.0),
                            [tok_eb[rs]] if tok_eb[rs] else [])
                        tok_rn[rs] = t2
                        t3 = bank_read(bk, lambda e, bk=bk, d=RB[rs]: e.tensor_scalar_max(
                            d, bk.t[:, :], 0.0), "v",
                            [t1] + ([tok_rb[rs]] if tok_rb[rs] else []))
                        t4 = v_op(lambda e, d=EB[rs], r=RB[rs]: e.tensor_add(d, d, r),
                                  [t2, t3])
                        tok_rb[rs] = t4
                        tok_eb[rs] = dma_gp(
                            g_q[rs],
                            q_s[:, h * TCORE + ga + co: h * TCORE + ga + co + 512],
                            EB[rs], [t4])
                    elif och < 32:
                        h = och - 16
                        dst = KG[:, h * TA + co: h * TA + co + 512]
                        t1 = bank_read(bk, lambda e, bk=bk, r=rn_f[rs]: e.activation(
                            r, bk.t[:, :], AF.Relu, scale=-1.0), "s",
                            [tok_rn[rs]] if tok_rn[rs] else [])
                        t2 = s_op(lambda e, r=rn_f[rs], d=dst: e.activation(
                            d, r, AF.Exp, scale=-1.0),
                            [tok_scan_all] if tok_scan_all else [])
                        tok_rn[rs] = t2
                        t3 = bank_read(bk, lambda e, bk=bk, d=RB[rs]: e.tensor_scalar_max(
                            d, bk.t[:, :], 0.0), "v",
                            [t1] + ([tok_rb[rs]] if tok_rb[rs] else []))
                        tok_rb[rs] = v_op(lambda e, d=dst, r=RB[rs]: e.tensor_add(d, d, r),
                                          [t2, t3])
                        if ts == 1:
                            k_tok[h] = tok_rb[rs]
                    else:
                        h = och - 32
                        dst = VG[:, h * TA + co: h * TA + co + 512]
                        tv = bank_read(bk, lambda e, bk=bk, d=dst: e.activation(
                            d, bk.t[:, :], AF.Copy), "s",
                            [tok_scan_all] if tok_scan_all else [])
                        if ts == 1:
                            v_tok[h] = tv
            if not (a_stop and a_stop.startswith("qkv")) and idx < 16 and idx % 2 == 1:
                emit_scans(idx // 2)
            if g < 2 and idx % 2 == 1 and idx < 24:
                ci = g * 12 + idx // 2
                r0, r1 = (ci * NN) // 24, ((ci + 1) * NN) // 24
                cast_toks["w0"].append(dma_gp(g_c0, w0_bf[r0:r1, :], w0[r0:r1, :]))
                q0, q1 = (ci * DM) // 24, ((ci + 1) * DM) // 24
                cast_toks["w1"].append(dma_gp(g_c1, w1_bf[q0:q1, :], w1[q0:q1, :]))
                if g == 1:
                    cj = idx // 2
                    o0, o1 = (cj * DM) // 12, ((cj + 1) * DM) // 12
                    cast_toks["wout"].append(dma_gp(g_cw, wout_bf[o0:o1, :],
                                                    w_out[o0:o1, :]))

        ev_tok = [(sch, sch.v), (vch, vch.v)]
        if dbg in ("k", "v") and g == 0:
            srcb = KG if dbg == "k" else VG
            for h in range(H):
                dbg_dump(srcb[:, h * TA: h * TA + TA], h * 128, 0, ev_tok)
        if a_stop and a_stop.startswith("qkv"):
            continue

        tok_scan_all = (vch, vch.v)

    if ngA < 2 or (a_stop and a_stop.startswith("qkv")):
        cast_toks["w0"].append(dma_gp(g_c0, w0_bf, w0))
        cast_toks["w1"].append(dma_gp(g_c1, w1_bf, w1))
        cast_toks["wout"].append(dma_gp(g_cw, wout_bf, w_out))
    tk_w0cast = (g_c0, g_c0.v)
    tk_w1cast = (g_c1, g_c1.v)
    tk_wout = (g_cw, g_cw.v)

    if not (ngA == 4 and a_stop is None):
        tp_add_w0([0, 1, 2, 3], {(nq, h): tk_w0cast for nq in range(4) for h in (0, 1)})
        tp_add_w1(tk_w1cast)
    while tp_state[0] < len(tp_pairs) or tp_pend:
        tp_emit(4)
    tkwt_all = [t for t in tps_store if t]

    exw = [t for t in (tok_scan_all, tk_consts) if t]
    for j in range(8):
        v_op(lambda e, j=j: e.tensor_scalar_mul(
            contrib[:, j * 32: j * 32 + 16], skv_st[:], onehot_s[:, j:j + 1]), exw)
        v_op(lambda e, j=j: e.tensor_scalar_mul(
            contrib[:, j * 32 + 16: j * 32 + 32], sks_st[:], onehot_s[:, j:j + 1]))
    tk_ccw = dma_gp(g_cc, cc_in.ap(), contrib, [(vch, vch.v)])
    if no_cc:
        tk_cc = dma_gp(g_cc, cc_out.ap(), cc_in.ap(), [tk_ccw])
    else:
        tk_cc = kb.op("gpsimd", lambda e: e.collective_compute(
            "AllReduce", ALU.add, replica_groups=[list(range(NCORES))],
            ins=[cc_in.ap().opt()], outs=[cc_out.ap().opt()]),
            [tk_ccw, (gch, gch.v)], inc=ccs, amt="default")
    tk_gath = dma_gp(g_cc, gath, cc_out.ap(), [tk_cc])
    g3 = gath.rearrange("p (c s) -> p c s", s=32)
    for s in range(32):
        v_op(lambda e, s=s: e.tensor_copy(slotm[:, s * 8: s * 8 + 8], g3[:, :, s]),
             [tk_gath])
    for s in range(32):
        v_op(lambda e, s=s: e.tensor_scalar_mul(
            wzf[:, s * 8: s * 8 + 8], ecnt_s[:, s * 8: s * 8 + 8],
            lndbc[:, (s % 16):(s % 16) + 1]), [tk_lndbc])
    tk_we = s_op(lambda e: e.activation(wzf[:], wzf[:], AF.Exp), [(vch, vch.v)])
    v_op(lambda e: e.tensor_mul(wzf[:], wzf[:], maskf_s[:]), [tk_we])
    v_op(lambda e: e.tensor_mul(slotm[:], slotm[:], wzf[:]))
    tk_sin = v_op(lambda e: e.reduce_sum(
        s_in[:], slotm.rearrange("p (s c) -> p s c", c=8), axis=AX.X))
    if dbg == "sin":
        dbg_dump(s_in[:], 0, 0, [tk_sin])

    QG = abf(B_QG, 8192)
    KVG = abf(B_KVG, 8192)
    KSG = abf(B_KSG, 8192)
    ATT = abf(B_ATT, 8192)
    WST = abf(B_WST, 16384)
    X3N = abf(B_X3N, 8192)
    EBF = abf(B_EBF, 8192)
    DGT = abf(B_DGT, 32768)
    W1P2 = [abf(B_W1P, 2048), abf(B_EBF + 6144, 2048)]
    TDN = [abf(B_WST + 8192 + i * 512, 512) for i in range(16)]
    TAP = [abf(B_ROT + i * 512, 512) for i in range(4)]
    R = afp(FB_R, 8192)

    tok_R = None
    tok_q = tok_kv = tok_ks = None
    tok_out_prev = None
    w1slot_pe = [None, None]
    wst_pe = [None, None]
    wst_dma = [None, None]
    tok_dbc = [None, None]
    tok_dgt_free = [None, None, None, None]
    tok_ffn1_done = None
    tok_tdn = [None] * 16
    tok_tap = [None, None, None, None]
    tok_wst_pe = None
    tok_dpw_pe = None
    tok_ffn2_pe = None
    tok_ebf_ln4 = None
    q3 = q_s.rearrange("p (h t) -> p h t", t=TCORE)
    kv3 = kv_s.rearrange("p (h t) -> p h t", t=TCORE)
    ks3 = ks_s.rearrange("p (h t) -> p h t", t=TCORE)
    last_scan_dma = [(g_q[0], g_q[0].v), (g_q[1], g_q[1].v), (g_kv, g_kv.v), (g_ks, g_ks.v)]
    tok_ffn2 = None
    tok_ln4_war = []

    def emit_attn_pre(g):
        nonlocal tok_q, tok_kv, tok_ks, tok_dbc, tok_den_pe, tok_rd_pe, tok_basep
        go = g * TB
        corr_tok = {}
        qtok = []
        for i in range(4):
            w = list(last_scan_dma) + [t for t in (tok_dgt_free[i], qtok_prev[i]) if t]
            dma_gp(qsem[i], QG[:, i * 2048: i * 2048 + 2048].rearrange("p (h t) -> p h t", t=TB),
                   q3[:, 4 * i: 4 * i + 4, go:go + TB], w)
            dma_gp(qsem[i], KVG[:, i * 2048: i * 2048 + 2048].rearrange("p (h t) -> p h t", t=TB),
                   kv3[:, 4 * i: 4 * i + 4, go:go + TB], [])
            tok = dma_gp(qsem[i], KSG[:, i * 2048: i * 2048 + 2048].rearrange("p (h t) -> p h t", t=TB),
                         ks3[:, 4 * i: 4 * i + 4, go:go + TB], [])
            qtok.append(tok)
            qtok_prev[i] = tok
        tok_q = tok_kv = tok_ks = qtok[3]
        if dbg == "kvloc" and g == 0:
            for h in range(H):
                dbg_dump(KVG[:, h * TB: h * TB + TB], h * 128, 0, [tok_kv])
        if dbg == "q" and g == 0:
            for h in range(H):
                dbg_dump(QG[:, h * TB: h * TB + TB], h * 128, 0, [tok_q])
        if b_stop == "loads":
            return {}

        dbc = [abf(AB_OFF + 1024 + i * 512, 512) for i in range(2)]
        dpwtmp = afp(S_WZ, 512)
        if tok_dbc[0] is None:
            tok_dbc = [tk_sin, tk_sin]
        exp_tok = {}
        t2_tok = {}
        tok_base = v_op(lambda e, go=go: e.tensor_scalar(
            dpwtmp, iota128, float(go + 1.0), None, ALU.add),
            [tk_consts, tk_lndbc, tk_l16, tk_sin] + ([tok_basep] if tok_basep else []))

        def emit_exp(h, extra):
            exp_tok[h] = s_op(lambda e, h=h, d=dbc[h % 2]: e.activation(
                d, dpwtmp, AF.Exp, scale=lndbc[:, h:h + 1]),
                [tok_base] + extra)
        for h in range(2):
            emit_exp(h, [tok_dbc[h % 2]] if tok_dbc[h % 2] else [])
        for h in range(H):
            sl = h % 2
            v_op(lambda e, h=h, d=dbc[sl]: e.scalar_tensor_tensor(
                KVG[:, h * TB: h * TB + TB], d, s_in[:, h:h + 1],
                KVG[:, h * TB: h * TB + TB], ALU.mult, ALU.add),
                [exp_tok[h], qtok[h // 4], tk_sin])
            tok_dbc[sl] = v_op(lambda e, h=h, d=dbc[sl]: e.scalar_tensor_tensor(
                KSG[:, h * TB: h * TB + TB], d, s_in[:, h + 16:h + 17],
                KSG[:, h * TB: h * TB + TB], ALU.mult, ALU.add), [])
            corr_tok[h] = tok_dbc[sl]
            if h + 2 < H:
                emit_exp(h + 2, [tok_dbc[sl]])
        tok_basep = exp_tok[H - 1]
        tok_corr = (vch, vch.v)
        if dbg == "kvc" and g == 0:
            for h in range(H):
                dbg_dump(KVG[:, h * TB: h * TB + TB], h * 128, 0, [tok_corr])
        if b_stop == "corr":
            return {}

        t1_tok = {}
        for h in range(H):
            t1_tok[h] = g_op(lambda e, h=h, d=TDN[h]: e.tensor_mul(
                d, QG[:, h * TB: h * TB + TB], KSG[:, h * TB: h * TB + TB]),
                [corr_tok[h], qtok[h // 4]]
                + ([tok_tdn[h]] if tok_tdn[h] else [])
                + ([wst_pe[1]] if wst_pe[1] else []))
            t2_tok[h] = g_op(lambda e, h=h: e.tensor_mul(
                QG[:, h * TB: h * TB + TB], QG[:, h * TB: h * TB + TB],
                KVG[:, h * TB: h * TB + TB]), [t1_tok[h]])
        return dict(g=g, t1_tok=t1_tok, t2_tok=t2_tok)

    def emit_attn_post(st):
        nonlocal tok_den_pe, tok_rd_pe
        if st is None or "t1_tok" not in st:
            return None, None
        g = st["g"]
        t1_tok, t2_tok = st["t1_tok"], st["t2_tok"]
        den_bk = banks[2]
        for h in range(H):
            w = (den_bk.rd + [t1_tok[h], tk_consts]) if h == 0 else [t1_tok[h]]
            tok_tdn[h] = t_mm(lambda e, h=h, d=TDN[h], b=den_bk: e.matmul(
                b.t[0:16, :], eones_s[:, h * 16: h * 16 + 16], d,
                start=(h == 0), stop=(h == H - 1), skip_group_check=True),
                w, mark=True)
        den_bk.wr, den_bk.rd = (pe, pe.v), []
        tok_den_pe = den_bk.wr
        bank_read(den_bk, lambda e, b=den_bk: e.tensor_scalar_max(
            den_f, b.t[0:16, :], 1e-6), "v", [tok_rd_pe] if tok_rd_pe else [])
        v_op(lambda e: e.reciprocal(recden, den_f))
        tok_rd = v_op(lambda e: e.tensor_copy(recden_bf, recden))
        if dbg == "den" and g == 0:
            dbg_dump(den_f, 0, 0, [(vch, vch.v)])
        if b_stop == "den":
            return None, None

        for h in range(H):
            bk = banks[3 + (h % 2)]
            mm_group(bk, [lambda e, bk=bk, h=h: e.matmul(
                bk.t[:, :], sel_s[:, h * 128: h * 128 + 128], recden_bf,
                start=True, stop=True)], [tok_rd])
            bank_read(bk, lambda e, bk=bk, h=h: e.tensor_mul(
                QG[:, h * TB: h * TB + TB], QG[:, h * TB: h * TB + TB], bk.t[:, :]),
                "v", [t2_tok[h]])
        tok_o = (vch, vch.v)
        if dbg == "o" and g == 0:
            for h in range(H):
                dbg_dump(QG[:, h * TB: h * TB + TB], h * 128, 0, [tok_o])
        tok_rd_pe = (pe, pe.v)
        tok_o = (vch, vch.v)
        if dbg == "o" and g == 0:
            for h in range(H):
                dbg_dump(QG[:, h * TB: h * TB + TB], h * 128, 0, [tok_o])
        ln4w = [b.wr for b in (banks[2], banks[3], banks[4], banks[5]) if b.wr]
        return tok_o, ln4w

    tok_den_pe = None
    tok_rd_pe = None
    tok_basep = None
    qtok_prev = [None, None, None, None]
    _st0 = emit_attn_pre(0)
    attn_res = emit_attn_post(_st0) if _st0 is not None else (None, None)
    for g in range(ngB):
        go = g * TB
        if attn_res is None or attn_res[0] is None:
            break
        tok_o, prev_ln_stats = attn_res
        wsl = 0
        wpw = [t for t in (wst_pe[0], tk_wout, wst_dma[0]) if t]
        if g == 0:
            wpw += [tok_scan_all, tk_sin]
        tk_wp0 = dma_sy(
            s_wst[0],
            WST[:, 0:8192].rearrange("p (c t) -> p c t", t=512),
            wout_bf.rearrange("(c p) n -> p c n", p=128)[:, :, 0:512],
            wpw)
        wst_dma[0] = tk_wp0
        tok_R = dma_gp(s_bR, R.rearrange("p (c t) -> p c t", t=TB), x3d[:, :, go:go + TB],
                       waits=list(last_scan_dma) + [t for t in (tok_out_prev, tok_R) if t]
                       + (tkwt_all if g == 0 else []))
        ln_bank_toks = [b.wr for b in (banks[2], banks[3], banks[4], banks[5]) if b.wr]
        att_evac = {}
        for oblk in range(4):
            wsl = oblk % 2
            if oblk == 0:
                tk_wp = tk_wp0
            else:
                wpw = [t for t in (wst_pe[wsl], tk_wout, wst_dma[wsl]) if t] +                     (ln_bank_toks if oblk < 2 else [])
                if oblk == 1:
                    wpw += [tok_den_pe]
                tk_wp = dma_sy(
                    s_wst[wsl],
                    WST[:, wsl * 8192: wsl * 8192 + 8192].rearrange("p (c t) -> p c t", t=512),
                    wout_bf.rearrange("(c p) n -> p c n", p=128)[:, :, oblk * 512: oblk * 512 + 512],
                    wpw)
                wst_dma[wsl] = tk_wp
            for foc in range(4):
                fo = oblk * 4 + foc
                bk = banks[fo % 2]
                mms = [(lambda e, bk=bk, fc=fc, foc=foc, wsl=wsl: e.matmul(
                    bk.t[:, :],
                    WST[:, wsl * 8192 + fc * 512 + foc * 128: wsl * 8192 + fc * 512 + foc * 128 + 128],
                    QG[:, fc * TB: fc * TB + TB],
                    start=(fc == 0), stop=(fc == NFC - 1))) for fc in range(NFC)]
                tok_wst_pe = mm_group(bk, mms, [tk_wp, tok_o])
                wst_pe[wsl] = tok_wst_pe
                att_evac[fo] = bank_read(bk, lambda e, bk=bk, fo=fo: e.activation(
                    ATT[:, fo * TB: fo * TB + TB], bk.t[:, :], AF.Copy), "s",
                    list(prev_ln_stats) if fo < 4 else [])
        tok_att = (sch, sch.v)
        if dbg == "attn" and g == 0:
            for fc in range(NFC):
                dbg_dump(ATT[:, fc * TB: fc * TB + TB], fc * 128, 0, [tok_att])
        if b_stop == "proj":
            continue

        tok_rows = ln_make(lambda fc: ATT[:, fc * TB: fc * TB + TB], TB,
                           [], B_KVG, deps_fc=[att_evac[fc] for fc in range(NFC)])
        a_bc, b_bc, tok_bc = ln_bcast(TB, tok_rows)
        tok_r1 = None
        r1_toks = []
        for fc in range(NFC):
            sl = fc % 4
            op = g_op if fc % 2 == 0 else v_op
            ta = op(lambda e, fc=fc, d=TAP[sl], a=a_bc, b=b_bc: e.tensor_mul(
                d, ATT[:, fc * TB: fc * TB + TB], a),
                [tok_bc] + ([tok_tap[sl]] if tok_tap[sl] else []))
            tb = op(lambda e, d=TAP[sl], b=b_bc: e.tensor_add(d, d, b), [ta])
            tok_tap[sl] = op(lambda e, fc=fc, d=TAP[sl]: e.tensor_add(
                R[:, fc * TB: fc * TB + TB], R[:, fc * TB: fc * TB + TB], d),
                [tok_R, tb])
            r1_toks.append(tok_tap[sl])
            tok_r1 = tok_tap[sl]
        if dbg == "r1" and g == 0:
            for fc in range(NFC):
                dbg_dump(R[:, fc * TB: fc * TB + TB], fc * 128, 0, [tok_r1])
        if b_stop == "ln2":
            continue

        tok_rows = ln_make(lambda fc: R[:, fc * TB: fc * TB + TB], TB,
                           [], B_ATT, src_is_f32=True, deps_fc=r1_toks,
                           csl_off=B_CSL)
        a_bc, b_bc, tok_bc = ln_bcast(TB, tok_rows)
        x3_toks = []
        for fc in range(NFC):
            sl = fc % 4
            op = g_op if fc % 2 == 0 else v_op
            ta = op(lambda e, fc=fc, d=TAP[sl], a=a_bc: e.tensor_mul(
                d, R[:, fc * TB: fc * TB + TB], a),
                [tok_bc] + ([tok_tap[sl]] if tok_tap[sl] else []))
            tok_tap[sl] = op(lambda e, fc=fc, d=TAP[sl], b=b_bc: e.tensor_add(
                X3N[:, fc * TB: fc * TB + TB], d, b), [ta] + list(tok_ln4_war))
            x3_toks.append(tok_tap[sl])
        tok_x3n = x3_toks[-1]
        if dbg == "x3n" and g == 0:
            for fc in range(NFC):
                dbg_dump(X3N[:, fc * TB: fc * TB + TB], fc * 128, 0, [tok_x3n])
        if b_stop == "ln3":
            continue

        w0t3 = w0t_s.rearrange("(f p) n -> p f n", p=128)
        for nblk in range(16):
            wsl = nblk % 2
            wpw = [t for t in (wst_pe[wsl], wst_dma[wsl]) if t] + tkwt_all
            tk_wp = dma_sy(
                s_wst[wsl],
                WST[:, wsl * 8192: wsl * 8192 + 8192].rearrange("p (c t) -> p c t", t=512),
                w0t3[:, :, nblk * 512: nblk * 512 + 512],
                wpw)
            wst_dma[wsl] = tk_wp
            for ncl in range(4):
                ncg = nblk * 4 + ncl
                bk = banks[ncl % 2]
                mms = [(lambda e, bk=bk, fc=fc, ncl=ncl, wsl=wsl: e.matmul(
                    bk.t[:, :],
                    WST[:, wsl * 8192 + fc * 512 + ncl * 128: wsl * 8192 + fc * 512 + ncl * 128 + 128],
                    X3N[:, fc * TB: fc * TB + TB],
                    start=(fc == 0), stop=(fc == NFC - 1))) for fc in range(NFC)]
                tok_wst_pe = mm_group(bk, mms, [tk_wp], mm_waits=x3_toks)
                wst_pe[wsl] = tok_wst_pe
                tok_dg = bank_read(bk, lambda e, bk=bk, ncg=ncg: e.activation(
                    DGT[:, ncg * TB: ncg * TB + TB], bk.t[:, :], AF.Gelu,
                    bias=b0r[:, ncg:ncg + 1]), "s")
        tok_dg_all = (sch, sch.v)
        tok_ffn1_done = (pe, pe.v)
        if dbg == "dg" and g == 0:
            for ncg in range(8):
                dbg_dump(DGT[:, ncg * TB: ncg * TB + TB], ncg * 128, 0, [tok_dg_all])
        if b_stop == "ffn1":
            continue

        w1t3 = w1t_s.rearrange("(n p) f -> p n f", p=128)
        fh1_order = []
        for i in range(4):
            fh1_order += [2 * i, 2 * i + 1, 8 + 2 * i, 8 + 2 * i + 1,
                          16 + 2 * i, 16 + 2 * i + 1]
        fh1_order += list(range(24, 32))
        ebf_evac = {}
        for fh in range(2):
            first_w = [[t for t in (banks[j].wr,) if t] + banks[j].rd for j in range(8)]
            for j in range(8):
                banks[j].rd = []
            for it2 in range(32):
                nc2 = it2 if fh == 0 else fh1_order[it2]
                sl = it2 % 2
                w = tkwt_all + ([w1slot_pe[sl]] if w1slot_pe[sl] else [])
                if sl == 1 and it2 == 1 and fh == 0:
                    w += [tok_ffn1_done]
                tok_w1dma = dma_sy(
                    s_w1p[sl],
                    W1P2[sl].rearrange("p (n f) -> p n f", f=1024),
                    w1t3[:, nc2 * 2: nc2 * 2 + 2, fh * 1024: fh * 1024 + 1024],
                    w)
                for k in range(2):
                    ncg = nc2 * 2 + k
                    for j in range(8):
                        w2 = ([tok_w1dma] + (first_w[j] if it2 == 0 else []) +
                              ([tok_dg_all] if it2 == 0 else []))
                        t_mm(lambda e, j=j, sl=sl, k=k, ncg=ncg, it2=it2: e.matmul(
                            PSn[j][:, :], W1P2[sl][:, k * 1024 + j * 128: k * 1024 + j * 128 + 128],
                            DGT[:, ncg * TB: ncg * TB + TB],
                            start=(it2 == 0 and k == 0), stop=(it2 == 31 and k == 1),
                            skip_group_check=True),
                            w2, mark=(j == 7 and k == 1))
                w1slot_pe[sl] = (pe, pe.v)
                if fh == 1 and it2 % 6 == 5 and it2 < 24:
                    tok_dgt_free[it2 // 6] = (pe, pe.v)
            for j in range(8):
                banks[j].wr, banks[j].rd = (pe, pe.v), []
            if fh == 0:
                for j in range(8):
                    ebf_evac[j] = bank_read(banks[j], lambda e, j=j: e.activation(
                        EBF[:, j * TB: j * TB + TB], PSn[j][:, :], AF.Identity,
                        bias=b1r[:, j:j + 1]), "s")
        st_next = emit_attn_pre(g + 1) if (g + 1 < ngB) else None
        for j in range(8):
            fo = 8 + j
            ebf_evac[fo] = bank_read(banks[j], lambda e, j=j, fo=fo: e.activation(
                EBF[:, fo * TB: fo * TB + TB], PSn[j][:, :], AF.Identity,
                bias=b1r[:, fo:fo + 1]), "s")
        tok_ebf = (sch, sch.v)
        tok_ffn2 = (pe, pe.v)
        attn_res = emit_attn_post(st_next) if st_next is not None else (None, None)
        if dbg == "e" and g == 0:
            for fo in range(NFC):
                dbg_dump(EBF[:, fo * TB: fo * TB + TB], fo * 128, 0, [tok_ebf])
        if b_stop == "ffn2":
            continue

        tok_rows = ln_make(lambda fc: EBF[:, fc * TB: fc * TB + TB], TB,
                           [], B_CSL, deps_fc=[ebf_evac[fc] for fc in range(NFC)])
        a_bc, b_bc, tok_bc = ln_bcast(TB, tok_rows)
        tok_r2 = None
        for fc in range(NFC):
            sl = fc % 4
            op = g_op if fc % 2 == 0 else v_op
            ta = op(lambda e, fc=fc, d=TAP[sl], a=a_bc: e.tensor_mul(
                d, EBF[:, fc * TB: fc * TB + TB], a),
                [tok_bc] + ([tok_tap[sl]] if tok_tap[sl] else []))
            tb = op(lambda e, d=TAP[sl], b=b_bc: e.tensor_add(d, d, b), [ta])
            tok_tap[sl] = op(lambda e, fc=fc, d=TAP[sl]: e.tensor_add(
                R[:, fc * TB: fc * TB + TB], R[:, fc * TB: fc * TB + TB], d), [tb])
            tok_r2 = tok_tap[sl]
        tok_ebf_ln4 = tok_r2
        tok_ln4_war = [tok_r2] + [b.wr for b in (banks[2], banks[3]) if b.wr]

        tok_out_prev = dma_out(
            y_t.rearrange("(c p) t -> p c t", p=128)[:, :, go:go + TB],
            R.rearrange("p (c t) -> p c t", t=TB), [tok_r2])

    kb.wait("sync", (sto, sto.v))
    for s in dma_sems:
        kb.wait("sync", (s, s.v))

    all_sems = dma_sems + [vch, sch, gch, pe, ccs]

    def emit_sem_reset():
        nums = sorted(s.h.num for s in all_sems)
        nc.all_engine_barrier()
        nc.gpsimd.dma_reset(range(nums[0], nums[-1] + 1))
        for s in all_sems:
            nc.gpsimd.sem_clear(s.h)
        nc.all_engine_barrier()

    emit_sem_reset()
    with nc.Block() as block:
        @block.sync
        def _(e):
            kb.replay("sync", e)

        @block.gpsimd
        def _(e):
            kb.replay("gpsimd", e)

        @block.scalar
        def _(e):
            kb.replay("scalar", e)

        @block.vector
        def _(e):
            kb.replay("vector", e)

        @block.tensor
        def _(e):
            kb.replay("tensor", e)

    emit_sem_reset()
    es.close()
    return nc


def _host_consts():
    sel = np.zeros((H, H * 128), np.float32)
    for h in range(H):
        sel[h, h * 128:(h + 1) * 128] = 1.0
    eones = np.zeros((128, H * 16), np.float32)
    for h in range(H):
        eones[:, h * 16 + h] = 1.0
    iota = np.broadcast_to(np.arange(TB, dtype=np.float32), (16, TB)).copy()
    iota128 = np.broadcast_to(np.arange(TB, dtype=np.float32), (128, TB)).copy()
    per_core = []
    for c in range(NCORES):
        ecnt = np.zeros((128, 256), np.float32)
        mask = np.zeros((128, 256), np.float32)
        oh = np.zeros((128, 8), np.float32)
        oh[:, c] = 1.0
        b = c // 4
        for s in range(32):
            for cp in range(8):
                if (cp // 4 == b) and (cp < c):
                    ecnt[:, s * 8 + cp] = TCORE * (c - 1 - cp)
                    mask[:, s * 8 + cp] = 1.0
        per_core.append((ecnt, mask, oh))
    return sel.astype(BF_NP), eones.astype(BF_NP), iota, iota128, per_core


def kernel(L0_attn_x, L0_attn_w_qkv, L0_attn_w_out, L0_attn_decay_param,
           L0_ffn_W0, L0_ffn_bias0, L0_ffn_W1, L0_ffn_bias1,
           _dbg=None, _ngA=NGA, _ngB=NGB, _no_cc=False, _a_stop=None, _b_stop=None):
    x = np.asarray(L0_attn_x, dtype=np.float32)
    wqkv = np.ascontiguousarray(np.asarray(L0_attn_w_qkv, dtype=np.float32))
    wout = np.ascontiguousarray(np.asarray(L0_attn_w_out, dtype=np.float32))
    w0 = np.ascontiguousarray(np.asarray(L0_ffn_W0, dtype=np.float32))
    w1 = np.ascontiguousarray(np.asarray(L0_ffn_W1, dtype=np.float32))
    dec = np.ascontiguousarray(np.asarray(L0_attn_decay_param, dtype=np.float32).reshape(1, H))
    b0r = np.ascontiguousarray(np.asarray(L0_ffn_bias0, dtype=np.float32).reshape(NNC, 128).T)
    b1r = np.ascontiguousarray(np.asarray(L0_ffn_bias1, dtype=np.float32).reshape(NFC, 128).T)
    sel, eones, iota, iota128, per_core = _host_consts()

    nc = build_nc(dbg=_dbg, ngA=_ngA, ngB=_ngB, no_cc=_no_cc, a_stop=_a_stop, b_stop=_b_stop)
    in_maps = []
    for c in range(NCORES):
        b, t0 = c // 4, (c % 4) * TCORE
        xt = np.ascontiguousarray(x[b, t0:t0 + TCORE, :].T)
        ecnt, mask, oh = per_core[c]
        in_maps.append(dict(
            x_t=xt, w_qkv=wqkv, w_out=wout, w0=w0, w1=w1, d_in=dec,
            b0r_in=b0r, b1r_in=b1r, sel_in=sel, eones_in=eones, iota_in=iota,
            iota128_in=iota128, ecnt_in=ecnt, maskf_in=mask, onehot_in=oh))
    res = run_bass_kernel_spmd(nc, in_maps, core_ids=list(range(NCORES)))
    out = np.empty((2, 16384, DM), np.float32)
    for c in range(NCORES):
        b, t0 = c // 4, (c % 4) * TCORE
        out[b, t0:t0 + TCORE, :] = res.results[c]["y_t"].T
    return out


# version 20
# speedup vs baseline: 1.0185x; 1.0185x over previous
import numpy as np
import ml_dtypes
from contextlib import ExitStack

import concourse.bass as bass
import concourse.mybir as mybir
from concourse.bass_utils import run_bass_kernel_spmd

F32 = mybir.dt.float32
BF16 = mybir.dt.bfloat16
AF = mybir.ActivationFunctionType
ALU = mybir.AluOpType
AX = mybir.AxisListType
BF_NP = ml_dtypes.bfloat16

NCORES = 8
DM = 2048
NFC = 16
H = 16
TCORE = 4096
TA = 1024
NGA = TCORE // TA
TB = 512
NGB = TCORE // TB
QKV = 6144
NN = 8192
NNC = NN // 128
EPS = np.float32(1e-5)
RDM = float(1.0 / DM)


class Sem:
    __slots__ = ("h", "name", "v")

    def __init__(self, h, name):
        self.h, self.name, self.v = h, name, 0

    def plan(self, amt):
        self.v += amt
        return self.v


class KB:
    ENGINES = ("sync", "gpsimd", "scalar", "vector", "tensor")

    def __init__(self):
        self.streams = {e: [] for e in self.ENGINES}
        self.lastw = {}

    def wait(self, stream, tok):
        if tok is None:
            return
        sem, val = tok
        if val is None or val <= 0:
            return
        key = (stream, sem.name)
        if self.lastw.get(key, 0) >= val:
            return
        self.lastw[key] = val
        self.streams[stream].append(("w", sem, val))

    def op(self, stream, fn, waits=(), inc=None, amt=1):
        for t in waits:
            self.wait(stream, t)
        if inc is not None:
            if amt == "default":
                tok = inc.plan(1)
                self.streams[stream].append(("d", fn, inc))
            else:
                tok = inc.plan(amt)
                self.streams[stream].append(("i", fn, inc, amt))
            return (inc, tok)
        self.streams[stream].append(("f", fn))
        return None

    def replay(self, stream, eng):
        for item in self.streams[stream]:
            if item[0] == "w":
                eng.wait_ge(item[1].h, item[2])
            elif item[0] == "i":
                item[1](eng).then_inc(item[2].h, item[3])
            elif item[0] == "d":
                item[1](eng).then_inc(item[2].h)
            else:
                item[1](eng)


def build_nc(dbg=None, ngA=NGA, ngB=NGB, no_cc=False, a_stop=None, b_stop=None):
    nc = bass.Bass("TRN2", target_bir_lowering=False, debug=False)

    x_t = nc.dram_tensor("x_t", [DM, TCORE], F32, kind="ExternalInput").ap()
    w_qkv = nc.dram_tensor("w_qkv", [DM, QKV], F32, kind="ExternalInput").ap()
    w_out = nc.dram_tensor("w_out", [DM, DM], F32, kind="ExternalInput").ap()
    w0 = nc.dram_tensor("w0", [NN, DM], F32, kind="ExternalInput").ap()
    w1 = nc.dram_tensor("w1", [DM, NN], F32, kind="ExternalInput").ap()
    d_in = nc.dram_tensor("d_in", [1, H], F32, kind="ExternalInput").ap()
    b0r_in = nc.dram_tensor("b0r_in", [128, NNC], F32, kind="ExternalInput").ap()
    b1r_in = nc.dram_tensor("b1r_in", [128, NFC], F32, kind="ExternalInput").ap()
    sel_in = nc.dram_tensor("sel_in", [H, H * 128], BF16, kind="ExternalInput").ap()
    eones_in = nc.dram_tensor("eones_in", [128, H * 16], BF16, kind="ExternalInput").ap()
    iota_in = nc.dram_tensor("iota_in", [16, TB], F32, kind="ExternalInput").ap()
    iota128_in = nc.dram_tensor("iota128_in", [128, TB], F32, kind="ExternalInput").ap()
    ecnt_in = nc.dram_tensor("ecnt_in", [128, 256], F32, kind="ExternalInput").ap()
    maskf_in = nc.dram_tensor("maskf_in", [128, 256], F32, kind="ExternalInput").ap()
    onehot_in = nc.dram_tensor("onehot_in", [128, 8], F32, kind="ExternalInput").ap()

    y_t = nc.dram_tensor("y_t", [DM, TCORE], F32, kind="ExternalOutput").ap()

    wout_bf = nc.dram_tensor("wout_bf", [DM, DM], BF16, kind="Internal").ap()
    wqkv_bf = nc.dram_tensor("wqkv_bf", [DM, QKV], BF16, kind="Internal").ap()
    w0_bf = nc.dram_tensor("w0_bf", [NN, DM], BF16, kind="Internal").ap()
    w1_bf = nc.dram_tensor("w1_bf", [DM, NN], BF16, kind="Internal").ap()
    w0t_s = nc.dram_tensor("w0t_s", [NFC * 128, NN], BF16, kind="Internal").ap()
    w1t_s = nc.dram_tensor("w1t_s", [NNC * 128, DM], BF16, kind="Internal").ap()
    q_s = nc.dram_tensor("q_s", [128, H * TCORE], BF16, kind="Internal").ap()
    kv_s = nc.dram_tensor("kv_s", [128, H * TCORE], BF16, kind="Internal").ap()
    ks_s = nc.dram_tensor("ks_s", [128, H * TCORE], BF16, kind="Internal").ap()
    lnd_d = nc.dram_tensor("lnd_d", [H, 1], F32, kind="Internal").ap()
    cc_in = nc.dram_tensor("cc_in", [128, 256], F32, kind="Internal")
    cc_out = nc.dram_tensor("cc_out", [128, 256], F32, kind="Internal")

    kb = KB()
    es = ExitStack()

    def sb(name, shape, dtype):
        return es.enter_context(nc.sbuf_tensor(name, shape, dtype))

    def mksem(name):
        return Sem(es.enter_context(nc.semaphore(name)), name)

    ABF = sb("ABF", [128, 64512], BF16)
    AFP = sb("AFP", [128, 20096], F32)

    def abf(off, n, p0=0, p1=128):
        return ABF[p0:p1, off:off + n]

    def afp(off, n, p0=0, p1=128):
        return AFP[p0:p1, off:off + n]

    A_XBF, A_KG, A_VG, A_WQP, A_ROT = 0, 16384, 32768, 49152, 57344
    A_XSQ = 59392
    AB_OFF = 62464
    assert AB_OFF + 2048 <= 64512
    B_QG, B_KVG, B_KSG, B_ATT = 0, 8192, 16384, 24576
    B_DGT = 0
    B_WST = 32768
    B_XSQ = B_WST + 8192
    B_X3N = 49152
    B_EBF = 49152
    B_W1P = 57344
    B_ROT = 59392
    B_CSL = 61440

    FA_SCANO, FA_U, FA_KF, FA_RN = 0, 2048, 3072, 4096
    FB_R = 0
    S_MU, S_M2, S_VAR, S_A, S_B = 8192, 9216, 10240, 11264, 12288
    S_IOTA, S_DPW, S_DEN, S_RECD = 13312, 13824, 14336, 14848
    S_ECNT, S_MASK, S_WZ, S_GATH, S_SLOT, S_CONTR = 15360, 15616, 15872, 16128, 16384, 16640
    S_SEL, S_EONES = 16896, 17920
    S_ARB, S_BRB = 18048, 18560
    S_DPWB, S_RECDB = 19072, 19328
    S_IOTA128 = 19584
    assert S_IOTA128 + 512 <= 20096

    mu_r = afp(S_MU, TA, 0, 1)
    m2_r = afp(S_M2, TA, 0, 1)
    var_r = afp(S_VAR, TA, 0, 1)
    a_r = afp(S_A, TA, 0, 1)
    b_r = afp(S_B, TA, 0, 1)
    iota16 = afp(S_IOTA, TB, 0, 16)
    dpw = afp(S_DPW, TB, 0, 16)
    den_f = afp(S_DEN, TB, 0, 16)
    recden = afp(S_RECD, TB, 0, 16)
    ecnt_s = afp(S_ECNT, 256)
    maskf_s = afp(S_MASK, 256)
    wzf = afp(S_WZ, 256)
    gath = afp(S_GATH, 256)
    slotm = afp(S_SLOT, 256)
    contrib = afp(S_CONTR, 256)
    sel_s = afp(S_SEL, 1024, 0, 16).bitcast(BF16)
    eones_s = afp(S_EONES, 128).bitcast(BF16)
    a_rb = afp(S_ARB, 512, 0, 1).bitcast(BF16)
    b_rb = afp(S_BRB, 512, 0, 1).bitcast(BF16)
    dpw_bf = afp(S_DPWB, 256, 0, 16).bitcast(BF16)
    iota128 = afp(S_IOTA128, TB)
    recden_bf = afp(S_RECDB, 256, 0, 16).bitcast(BF16)

    dec_pc = sb("dec_pc", [128, H], F32)
    lndbc = sb("lndbc", [128, H], F32)
    skv_st = sb("skv_st", [128, H], F32)
    sks_st = sb("sks_st", [128, H], F32)
    s_in = sb("s_in", [128, 32], F32)
    onehot_s = sb("onehot_s", [128, 8], F32)
    b0r = sb("b0r", [128, NNC], F32)
    b1r = sb("b1r", [128, NFC], F32)
    drow = sb("drow", [1, H], F32)
    lnd_row = sb("lnd_row", [1, H], F32)
    lnd16 = sb("lnd16", [H, 1], F32)
    ones1b = sb("ones1b", [1, 128], BF16)
    onescb = sb("onescb", [128, 1], BF16)
    ones1f = sb("ones1f", [1, 128], F32)
    PSn = [es.enter_context(nc.psum_tensor(f"ps{i}", [128, 512], F32)) for i in range(8)]

    s_con = mksem("s_con")
    s_wqp = [mksem("s_wqp0"), mksem("s_wqp1")]
    s_wst = [mksem("s_wst0"), mksem("s_wst1")]
    s_w1p = [mksem("s_w1p0"), mksem("s_w1p1")]
    s_bR = mksem("s_bR")
    qsem = [mksem(f"qsem{i}") for i in range(4)]
    tp_sem = [mksem(f"s_tp{b}") for b in range(6)]
    g_cwq4 = [mksem(f"g_cwq{i}") for i in range(4)]
    g_cw = mksem("g_cw")
    g_c0 = mksem("g_c0")
    g_c1 = mksem("g_c1")
    g_x = mksem("g_x")
    g_q = [mksem("g_q0"), mksem("g_q1")]
    g_kv = mksem("g_kv")
    g_ks = mksem("g_ks")
    g_cc = mksem("g_cc")
    g_mi = mksem("g_mi")
    g_dbg = mksem("g_dbg")
    vch = mksem("vch")
    sch = mksem("sch")
    gch = mksem("gch")
    pe = mksem("pe")
    ccs = mksem("ccs")
    sto = mksem("sto")
    dma_sems = [s_con, s_wqp[0], s_wqp[1], s_wst[0], s_wst[1], s_w1p[0], s_w1p[1],
                s_bR, qsem[0], qsem[1], qsem[2], qsem[3],
                tp_sem[0], tp_sem[1], tp_sem[2],
                tp_sem[3], tp_sem[4], tp_sem[5],
                g_cwq4[0], g_cwq4[1], g_cwq4[2], g_cwq4[3], g_cw,
                g_c0, g_c1, g_x, g_q[0], g_q[1], g_kv, g_ks, g_cc, g_mi, g_dbg, sto]

    def v_op(fn, waits=()):
        return kb.op("vector", fn, list(waits) + [(vch, vch.v)], inc=vch)

    def s_op(fn, waits=()):
        return kb.op("scalar", fn, list(waits) + [(sch, sch.v)], inc=sch)

    def g_op(fn, waits=()):
        return kb.op("gpsimd", fn, list(waits) + [(gch, gch.v)], inc=gch)

    def t_mm(fn, waits=(), mark=False):
        return kb.op("tensor", fn, waits, inc=pe if mark else None)

    def dma_sy(sem, dst, src, waits=(), **kw):
        return kb.op("sync", lambda e, d=dst, s=src, kw=kw: e.dma_start(d, s, **kw),
                     waits, inc=sem, amt=16)

    def dma_out(dst, src, waits=()):
        return kb.op("gpsimd", lambda e, d=dst, s=src: e.dma_start(d, s),
                     waits, inc=sto, amt=16)

    def dma_gp(sem, dst, src, waits=(), **kw):
        return kb.op("gpsimd", lambda e, d=dst, s=src, kw=kw: e.dma_start(d, s, **kw),
                     waits, inc=sem, amt=16)

    class PBank:
        __slots__ = ("i", "wr", "rd")

        def __init__(self, i):
            self.i, self.wr, self.rd = i, None, []

        @property
        def t(self):
            return PSn[self.i]

    banks = [PBank(i) for i in range(8)]

    def mm_group(bank, mms, waits=(), mm_waits=None):
        w = list(waits) + bank.rd
        tok = None
        n = len(mms)
        for j, fn in enumerate(mms):
            wj = list(w) if j == 0 else []
            if mm_waits is not None and mm_waits[j] is not None:
                wj = wj + [mm_waits[j]]
            tok = t_mm(fn, wj, mark=(j == n - 1))
        bank.wr = tok
        bank.rd = []
        return tok

    def bank_read(bank, emit, kind="v", waits=()):
        w = list(waits) + ([bank.wr] if bank.wr else [])
        tok = v_op(emit, w) if kind == "v" else s_op(emit, w)
        bank.rd.append(tok)
        return tok

    dbg_toks = []

    def dbg_dump(ap, r0=0, c0=0, wait=()):
        dst = y_t[r0:r0 + ap.shape[0], c0:c0 + ap.shape[-1]]
        if ap.dtype == F32:
            dbg_toks.append(dma_sy(g_dbg, dst, ap, waits=list(wait)))
        else:
            dbg_toks.append(dma_gp(g_dbg, dst, ap, waits=list(wait)))

    tk_ones = g_op(lambda e: e.memset(ones1b[:], 1.0))
    tk_onescb = g_op(lambda e: e.memset(onescb[:], 1.0))
    tk_o1f = g_op(lambda e: e.memset(ones1f[:], 1.0))
    g_op(lambda e: e.memset(skv_st[:], 0.0))
    tk_st0 = g_op(lambda e: e.memset(sks_st[:], 0.0))

    dma_sy(s_con, drow[:], d_in)
    dma_sy(s_con, b0r[:], b0r_in)
    dma_sy(s_con, b1r[:], b1r_in)
    dma_sy(s_con, sel_s, sel_in)
    dma_sy(s_con, eones_s, eones_in)
    dma_sy(s_con, iota16, iota_in)
    dma_sy(s_con, iota128, iota128_in)
    dma_sy(s_con, ecnt_s, ecnt_in)
    dma_sy(s_con, maskf_s, maskf_in)
    dma_sy(s_con, onehot_s[:], onehot_in)
    tk_consts = (s_con, s_con.v)
    tk_d = tk_consts


    tk_sg = s_op(lambda e: e.activation(drow[:], drow[:], AF.Sigmoid), [tk_d])
    mm_group(banks[0], [lambda e: e.matmul(PSn[0][0:128, 0:H], ones1f[:], drow[:],
                                           start=True, stop=True)], [tk_sg, tk_o1f])
    tk_decpc = bank_read(banks[0], lambda e: e.tensor_copy(dec_pc[:], PSn[0][0:128, 0:H]))
    tk_lnr = s_op(lambda e: e.activation(lnd_row[:], drow[:], AF.Ln), [tk_sg])
    tk_lnc = v_op(lambda e: e.tensor_scalar_max(lnd_row[:], lnd_row[:], -87.0), [tk_lnr])
    mm_group(banks[1], [lambda e: e.matmul(PSn[1][0:128, 0:H], ones1f[:], lnd_row[:],
                                           start=True, stop=True)], [tk_lnc, tk_o1f])
    tk_lndbc = bank_read(banks[1], lambda e: e.tensor_copy(lndbc[:], PSn[1][0:128, 0:H]))
    tk_l2d = dma_gp(g_mi, lnd_d.rearrange("h one -> one h"), lnd_row[:], [tk_lnc])
    tk_l16 = dma_gp(g_mi, lnd16[:], lnd_d, [tk_l2d], allow_slow_non_contiguous=True)

    def ln_make(src_chunk, T, deps, xsq_off, src_is_f32=False, deps_fc=None,
                csl_off=None):
        nts = T // 512
        sum_bk = [banks[2], banks[3]]
        sq_bk = [banks[4], banks[5]]
        slot_tok = [None, None]
        for fc in range(NFC):
            src = src_chunk(fc)
            slot = abf(xsq_off + (fc % 2) * T, T)
            w = list(deps) + ([slot_tok[fc % 2]] if slot_tok[fc % 2] else [])
            if deps_fc is not None and deps_fc[fc] is not None:
                w = w + [deps_fc[fc]]
            tok_sq = s_op(lambda e, slot=slot, src=src: e.activation(slot, src, AF.Square), w)
            if src_is_f32:
                co = csl_off if csl_off is not None else (xsq_off + 2 * T)
                cslot = abf(co + (fc % 2) * T, T)
                tok_cp = s_op(lambda e, cslot=cslot, src=src: e.copy(cslot, src))
                sum_src = cslot
            else:
                tok_cp = None
                sum_src = src
            tok_q = None
            for ts in range(nts):
                bs, bq = sum_bk[ts], sq_bk[ts]
                wsum = (list(deps) + bs.rd + [tk_onescb]) if fc == 0 else []
                if deps_fc is not None and deps_fc[fc] is not None:
                    wsum = wsum + [deps_fc[fc]]
                if tok_cp:
                    wsum = wsum + [tok_cp]
                t_mm(lambda e, bs=bs, s2=sum_src, o=ts * 512, fc=fc: e.matmul(
                    bs.t[0:1, :], onescb[:], s2[:, o:o + 512],
                    start=(fc == 0), stop=(fc == NFC - 1), skip_group_check=True),
                    wsum, mark=(fc == NFC - 1))
                tok_s = (pe, pe.v) if fc == NFC - 1 else None
                if tok_s:
                    bs.wr, bs.rd = tok_s, []
                wq = [tok_sq] + (bq.rd if fc == 0 else [])
                t_mm(lambda e, bq=bq, sl=slot, o=ts * 512, fc=fc: e.matmul(
                    bq.t[0:1, :], onescb[:], sl[:, o:o + 512],
                    start=(fc == 0), stop=(fc == NFC - 1), skip_group_check=True),
                    wq, mark=True)
                tok_q = (pe, pe.v)
                if fc == NFC - 1:
                    bq.wr, bq.rd = tok_q, []
            slot_tok[fc % 2] = tok_q
        for ts in range(nts):
            o = ts * 512
            bank_read(sum_bk[ts], lambda e, o=o, b=sum_bk[ts]: e.tensor_scalar_mul(
                mu_r[:, o:o + 512], b.t[0:1, :], RDM))
            bank_read(sq_bk[ts], lambda e, o=o, b=sq_bk[ts]: e.tensor_scalar_mul(
                m2_r[:, o:o + 512], b.t[0:1, :], RDM))
        v_op(lambda e: e.tensor_mul(var_r[:, 0:T], mu_r[:, 0:T], mu_r[:, 0:T]))
        tok_v = v_op(lambda e: e.tensor_sub(var_r[:, 0:T], m2_r[:, 0:T], var_r[:, 0:T]))
        s_op(lambda e: e.activation(var_r[:, 0:T], var_r[:, 0:T], AF.Ln, bias=EPS), [tok_v])
        tk_a = s_op(lambda e: e.activation(a_r[:, 0:T], var_r[:, 0:T], AF.Exp, scale=-0.5))
        v_op(lambda e: e.scalar_tensor_tensor(b_r[:, 0:T], mu_r[:, 0:T], -1.0,
                                              a_r[:, 0:T], ALU.mult, ALU.mult), [tk_a])
        v_op(lambda e: e.tensor_copy(a_rb[:, 0:T], a_r[:, 0:T]))
        tok_rows = v_op(lambda e: e.tensor_copy(b_rb[:, 0:T], b_r[:, 0:T]))
        return tok_rows

    def ln_bcast(T, tok_rows):
        a_bc = abf(AB_OFF, T)
        b_bc = abf(AB_OFF + T, T)
        tok = None
        for which, row, dst in ((0, a_rb, a_bc), (1, b_rb, b_bc)):
            for ts in range(T // 512):
                bk = banks[6 + (which + ts) % 2]
                mm_group(bk, [lambda e, bk=bk, row=row, o=ts * 512: e.matmul(
                    bk.t[0:128, 0:512], ones1b[:], row[:, o:o + 512],
                    start=True, stop=True)], [tok_rows, tk_ones])
                tok = bank_read(bk, lambda e, bk=bk, dst=dst, o=ts * 512:
                                e.tensor_copy(dst[:, o:o + 512], bk.t[0:128, 0:512]))
        return a_bc, b_bc, tok

    X = abf(A_XBF, 16384)
    KG = abf(A_KG, 16384)
    VG = abf(A_VG, 16384)
    WQPs = [abf(A_WQP, 4096), abf(A_WQP + 4096, 4096)]
    TPS = [afp(5120 + i * 512, 512).bitcast(BF16) for i in range(6)]
    EB = [abf(A_ROT + i * 512, 512) for i in range(2)]
    RB = [abf(A_ROT + 1024 + i * 512, 512) for i in range(2)]
    scano = [afp(FA_SCANO + i * 1024, 1024) for i in range(2)]
    u_f = afp(FA_U, 1024)
    kf_f = afp(FA_KF, 1024)
    rn_f = [afp(FA_RN + i * 512, 512) for i in range(2)]

    x3d = x_t.rearrange("(c p) t -> p c t", p=128)
    tk_X = None
    tok_qkv_last = None
    wqp_pe = [None, None]
    piece_tok = [None, None]
    cast_toks = {"wout": [], "w0": [], "w1": []}
    wq_cast = {}
    wq_prev = [None, None, None, None]
    tok_scan_all = None
    tok_eb = [None, None]
    tok_rb = [None, None]
    tok_rn = [None, None]
    tok_scano = [None, None]
    tok_sdma = [None, None]
    tok_u = None
    tok_kf = None
    tok_wqp_pe = None

    tp_pairs = []
    tp_state = [0]
    tp_pend = []
    tps_store = [None] * 6
    tp_nbuf = [0]

    def tp_add_w0(nqs, deps):
        for nq in nqs:
            for h in range(2):
                dep = deps[(nq, h)]
                r0 = nq * 2048 + h * 1024
                for fc in range(NFC):
                    tp_pairs.append((
                        w0_bf[r0: r0 + 1024, fc * 128: fc * 128 + 128],
                        w0t_s[fc * 128: fc * 128 + 128, r0: r0 + 1024],
                        dep))

    def tp_add_w1(dep):
        for ncg in range(NNC):
            for h in range(2):
                tp_pairs.append((
                    w1_bf[h * 1024: h * 1024 + 1024, ncg * 128: ncg * 128 + 128],
                    w1t_s[ncg * 128: ncg * 128 + 128, h * 1024: h * 1024 + 1024],
                    dep))

    def dma_sc(sem, dst, src, waits=(), **kw):
        return kb.op("scalar", lambda e, d=dst, s=src, kw=kw: e.dma_start(d, s, **kw),
                     waits, inc=sem, amt=16)

    def tp_flush():
        for bufslot, ttok, dst in tp_pend:
            tps_store[bufslot] = dma_sy(tp_sem[bufslot], dst, TPS[bufslot], [ttok])
        del tp_pend[:]

    def tp_emit(n):
        tp_flush()
        while n > 0 and tp_state[0] < len(tp_pairs):
            src, dst, dep = tp_pairs[tp_state[0]]
            bufslot = tp_nbuf[0] % 6
            tp_nbuf[0] += 1
            w = [t for t in (dep, tps_store[bufslot]) if t]
            ttok = dma_sy(tp_sem[bufslot], TPS[bufslot], src, w, transpose=True)
            tp_pend.append((bufslot, ttok, dst))
            tp_state[0] += 1
            n -= 1

    for g in range(ngA):
        ga = g * TA
        if ngA == 4 and a_stop is None:
            if g == 1:
                tp_add_w0([0, 1], {(nq, h): cast_toks["w0"][3 * (2 * nq + h + 1) - 1]
                                   for nq in (0, 1) for h in (0, 1)})
            elif g == 2:
                tp_add_w0([2, 3], {(nq, h): cast_toks["w0"][3 * (2 * nq + h + 1) - 1]
                                   for nq in (2, 3) for h in (0, 1)})
                tp_add_w1((g_c1, g_c1.v))
        x3v = X.rearrange("p (c t) -> p c t", t=TA)
        xq_tok = []
        for xq in range(4):
            t = dma_gp(g_x, x3v[:, 4 * xq: 4 * xq + 4, :],
                       x3d[:, 4 * xq: 4 * xq + 4, ga:ga + TA],
                       waits=([t for t in (tok_qkv_last, tk_X) if t] if xq == 0 else []))
            xq_tok.append(t)
        tk_X = xq_tok[3]

        tok_rows = ln_make(lambda fc: X[:, fc * TA: fc * TA + TA], TA,
                           [tk_consts], A_XSQ,
                           deps_fc=[xq_tok[fc // 4] for fc in range(NFC)])
        a_bc, b_bc, tok_bc = ln_bcast(TA, tok_rows)
        stat_toks = [b.wr for b in (banks[2], banks[3], banks[4], banks[5]) if b.wr]
        apply_toks = []
        for fc in range(NFC):
            sl = X[:, fc * TA: fc * TA + TA]
            op = g_op if fc % 2 == 0 else v_op
            op(lambda e, sl=sl, a=a_bc: e.tensor_mul(sl, sl, a), [tok_bc] + stat_toks)
            apply_toks.append(op(lambda e, sl=sl, b=b_bc: e.tensor_add(sl, sl, b)))
        tok_apply = apply_toks[-1]

        if dbg == "xn" and g == 0:
            for fc in range(NFC):
                dbg_dump(X[:, fc * TA: fc * TA + TA], fc * 128, 0, [tok_apply])
        if dbg == "ln1rows" and g == 0:
            dbg_dump(mu_r[:, 0:TA], 0, 0, [tok_rows])
            dbg_dump(m2_r[:, 0:TA], 1, 0, [tok_rows])
            dbg_dump(a_r[:, 0:TA], 2, 0, [tok_rows])
            dbg_dump(b_r[:, 0:TA], 3, 0, [tok_rows])
        if a_stop == "ln1":
            continue

        n_oblk = 24 if a_stop not in ("qkv1", "qkv2", "qkv6") else 2 * int(a_stop[3])
        oblk_order = []
        for i in range(8):
            oblk_order += [8 + i, 16 + i]
        oblk_order += list(range(8))
        k_tok = {}
        v_tok = {}

        def emit_wq_cast(idx2):
            if g == 0 and idx2 < 24:
                ob = oblk_order[idx2]
                par = idx2 % 4
                wq_cast[ob] = dma_gp(
                    g_cwq4[par], wqkv_bf[:, ob * 256: ob * 256 + 256],
                    w_qkv[:, ob * 256: ob * 256 + 256],
                    [wq_prev[par]] if wq_prev[par] else [])
                wq_prev[par] = wq_cast[ob]
        if g == 0:
            for i2 in range(4):
                emit_wq_cast(i2)

        def emit_scans(hpair):
            nonlocal tok_u, tok_kf
            for h in (2 * hpair, 2 * hpair + 1):
                tok_u = v_op(lambda e, h=h: e.tensor_mul(
                    u_f, KG[:, h * TA: h * TA + TA], VG[:, h * TA: h * TA + TA]),
                    [k_tok[h], v_tok[h]] + ([tok_u] if tok_u else []))
                tk_s1 = v_op(lambda e, h=h, d=scano[0]: e.tensor_tensor_scan(
                    d, dec_pc[:, h:h + 1].broadcast_to([128, TA]), u_f,
                    skv_st[:, h:h + 1], ALU.mult, ALU.add),
                    [tok_u, tk_decpc] + [t for t in (tok_sdma[0],) if t])
                v_op(lambda e, h=h, d=scano[0]: e.tensor_copy(
                    skv_st[:, h:h + 1], d[:, TA - 1:TA]), [tk_s1])
                tok_sdma[0] = dma_gp(g_kv, kv_s[:, h * TCORE + ga: h * TCORE + ga + TA],
                                     scano[0], [(vch, vch.v)])
                tok_kf = v_op(lambda e, h=h: e.tensor_copy(
                    kf_f, KG[:, h * TA: h * TA + TA]),
                    [k_tok[h]] + ([tok_kf] if tok_kf else []))
                tk_s2 = v_op(lambda e, h=h, d=scano[1]: e.tensor_tensor_scan(
                    d, dec_pc[:, h:h + 1].broadcast_to([128, TA]), kf_f,
                    sks_st[:, h:h + 1], ALU.mult, ALU.add),
                    [tok_kf, tk_st0] + [t for t in (tok_sdma[1],) if t])
                v_op(lambda e, h=h, d=scano[1]: e.tensor_copy(
                    sks_st[:, h:h + 1], d[:, TA - 1:TA]), [tk_s2])
                tok_sdma[1] = dma_gp(g_ks, ks_s[:, h * TCORE + ga: h * TCORE + ga + TA],
                                     scano[1], [(vch, vch.v)])
                tok_u = tk_s1
                tok_kf = tk_s2

        for idx in range(24):
            if idx >= n_oblk:
                break
            oblk = oblk_order[idx]
            psl = idx % 2
            wqp_w = [t for t in (wqp_pe[psl], piece_tok[psl]) if t]
            if oblk in wq_cast:
                wqp_w.append(wq_cast[oblk])
            tk_piece = dma_sy(
                s_wqp[psl],
                WQPs[psl].rearrange("p (c t) -> p c t", t=256),
                wqkv_bf.rearrange("(c p) n -> p c n", p=128)[:, :, oblk * 256:oblk * 256 + 256],
                waits=wqp_w)
            piece_tok[psl] = tk_piece
            emit_wq_cast(idx + 4)
            if g >= 1:
                tp_emit(4)
            for oc in range(2):
                och = oblk * 2 + oc
                for ts in range(2):
                    bk = banks[(och * 2 + ts) % 2]
                    mms = [(lambda e, bk=bk, fc=fc, oc=oc, ts=ts, psl=psl: e.matmul(
                        bk.t[:, :],
                        WQPs[psl][:, fc * 256 + oc * 128: fc * 256 + oc * 128 + 128],
                        X[:, fc * TA + ts * 512: fc * TA + ts * 512 + 512],
                        start=(fc == 0), stop=(fc == NFC - 1)))
                        for fc in range(NFC)]
                    tok_qkv_last = mm_group(bk, mms, [tk_piece],
                                            mm_waits=apply_toks)
                    wqp_pe[psl] = tok_qkv_last
                    co = ts * 512
                    rs = (och * 2 + ts) % 2
                    if och < 16:
                        h = och
                        t1 = bank_read(bk, lambda e, bk=bk, r=rn_f[rs]: e.activation(
                            r, bk.t[:, :], AF.Relu, scale=-1.0), "s",
                            [tok_rn[rs]] if tok_rn[rs] else [])
                        t2 = s_op(lambda e, r=rn_f[rs], d=EB[rs]: e.activation(
                            d, r, AF.Exp, scale=-1.0),
                            [tok_eb[rs]] if tok_eb[rs] else [])
                        tok_rn[rs] = t2
                        t3 = bank_read(bk, lambda e, bk=bk, d=RB[rs]: e.tensor_scalar_max(
                            d, bk.t[:, :], 0.0), "v",
                            [t1] + ([tok_rb[rs]] if tok_rb[rs] else []))
                        t4 = v_op(lambda e, d=EB[rs], r=RB[rs]: e.tensor_add(d, d, r),
                                  [t2, t3])
                        tok_rb[rs] = t4
                        tok_eb[rs] = dma_gp(
                            g_q[rs],
                            q_s[:, h * TCORE + ga + co: h * TCORE + ga + co + 512],
                            EB[rs], [t4])
                    elif och < 32:
                        h = och - 16
                        dst = KG[:, h * TA + co: h * TA + co + 512]
                        t1 = bank_read(bk, lambda e, bk=bk, r=rn_f[rs]: e.activation(
                            r, bk.t[:, :], AF.Relu, scale=-1.0), "s",
                            [tok_rn[rs]] if tok_rn[rs] else [])
                        t2 = s_op(lambda e, r=rn_f[rs], d=dst: e.activation(
                            d, r, AF.Exp, scale=-1.0),
                            [tok_scan_all] if tok_scan_all else [])
                        tok_rn[rs] = t2
                        t3 = bank_read(bk, lambda e, bk=bk, d=RB[rs]: e.tensor_scalar_max(
                            d, bk.t[:, :], 0.0), "v",
                            [t1] + ([tok_rb[rs]] if tok_rb[rs] else []))
                        tok_rb[rs] = v_op(lambda e, d=dst, r=RB[rs]: e.tensor_add(d, d, r),
                                          [t2, t3])
                        if ts == 1:
                            k_tok[h] = tok_rb[rs]
                    else:
                        h = och - 32
                        dst = VG[:, h * TA + co: h * TA + co + 512]
                        tv = bank_read(bk, lambda e, bk=bk, d=dst: e.activation(
                            d, bk.t[:, :], AF.Copy), "s",
                            [tok_scan_all] if tok_scan_all else [])
                        if ts == 1:
                            v_tok[h] = tv
            if not (a_stop and a_stop.startswith("qkv")) and idx < 16 and idx % 2 == 1:
                emit_scans(idx // 2)
            if g < 2 and idx % 2 == 1 and idx < 24:
                ci = g * 12 + idx // 2
                r0, r1 = (ci * NN) // 24, ((ci + 1) * NN) // 24
                cast_toks["w0"].append(dma_gp(g_c0, w0_bf[r0:r1, :], w0[r0:r1, :]))
                q0, q1 = (ci * DM) // 24, ((ci + 1) * DM) // 24
                cast_toks["w1"].append(dma_gp(g_c1, w1_bf[q0:q1, :], w1[q0:q1, :]))
                if g == 1:
                    cj = idx // 2
                    o0, o1 = (cj * DM) // 12, ((cj + 1) * DM) // 12
                    cast_toks["wout"].append(dma_gp(g_cw, wout_bf[o0:o1, :],
                                                    w_out[o0:o1, :]))

        ev_tok = [(sch, sch.v), (vch, vch.v)]
        if dbg in ("k", "v") and g == 0:
            srcb = KG if dbg == "k" else VG
            for h in range(H):
                dbg_dump(srcb[:, h * TA: h * TA + TA], h * 128, 0, ev_tok)
        if a_stop and a_stop.startswith("qkv"):
            continue

        tok_scan_all = (vch, vch.v)

    if ngA < 2 or (a_stop and a_stop.startswith("qkv")):
        cast_toks["w0"].append(dma_gp(g_c0, w0_bf, w0))
        cast_toks["w1"].append(dma_gp(g_c1, w1_bf, w1))
        cast_toks["wout"].append(dma_gp(g_cw, wout_bf, w_out))
    tk_w0cast = (g_c0, g_c0.v)
    tk_w1cast = (g_c1, g_c1.v)
    tk_wout = (g_cw, g_cw.v)

    if not (ngA == 4 and a_stop is None):
        tp_add_w0([0, 1, 2, 3], {(nq, h): tk_w0cast for nq in range(4) for h in (0, 1)})
        tp_add_w1(tk_w1cast)
    while tp_state[0] < len(tp_pairs) or tp_pend:
        tp_emit(4)
    tkwt_all = [t for t in tps_store if t]

    exw = [t for t in (tok_scan_all, tk_consts) if t]
    for j in range(8):
        v_op(lambda e, j=j: e.tensor_scalar_mul(
            contrib[:, j * 32: j * 32 + 16], skv_st[:], onehot_s[:, j:j + 1]), exw)
        v_op(lambda e, j=j: e.tensor_scalar_mul(
            contrib[:, j * 32 + 16: j * 32 + 32], sks_st[:], onehot_s[:, j:j + 1]))
    tk_ccw = dma_gp(g_cc, cc_in.ap(), contrib, [(vch, vch.v)])
    if no_cc:
        tk_cc = dma_gp(g_cc, cc_out.ap(), cc_in.ap(), [tk_ccw])
    else:
        tk_cc = kb.op("gpsimd", lambda e: e.collective_compute(
            "AllReduce", ALU.add, replica_groups=[list(range(NCORES))],
            ins=[cc_in.ap().opt()], outs=[cc_out.ap().opt()]),
            [tk_ccw, (gch, gch.v)], inc=ccs, amt="default")
    tk_gath = dma_gp(g_cc, gath, cc_out.ap(), [tk_cc])
    g3 = gath.rearrange("p (c s) -> p c s", s=32)
    for s in range(32):
        v_op(lambda e, s=s: e.tensor_copy(slotm[:, s * 8: s * 8 + 8], g3[:, :, s]),
             [tk_gath])
    for s in range(32):
        v_op(lambda e, s=s: e.tensor_scalar_mul(
            wzf[:, s * 8: s * 8 + 8], ecnt_s[:, s * 8: s * 8 + 8],
            lndbc[:, (s % 16):(s % 16) + 1]), [tk_lndbc])
    tk_we = s_op(lambda e: e.activation(wzf[:], wzf[:], AF.Exp), [(vch, vch.v)])
    v_op(lambda e: e.tensor_mul(wzf[:], wzf[:], maskf_s[:]), [tk_we])
    v_op(lambda e: e.tensor_mul(slotm[:], slotm[:], wzf[:]))
    tk_sin = v_op(lambda e: e.reduce_sum(
        s_in[:], slotm.rearrange("p (s c) -> p s c", c=8), axis=AX.X))
    if dbg == "sin":
        dbg_dump(s_in[:], 0, 0, [tk_sin])

    QG = abf(B_QG, 8192)
    KVG = abf(B_KVG, 8192)
    KSG = abf(B_KSG, 8192)
    ATT = abf(B_ATT, 8192)
    WST = abf(B_WST, 16384)
    X3N = abf(B_X3N, 8192)
    EBF = abf(B_EBF, 8192)
    DGT = abf(B_DGT, 32768)
    W1P2 = [abf(B_W1P, 2048), abf(B_EBF + 6144, 2048)]
    TDN = [abf(B_WST + 8192 + i * 512, 512) for i in range(16)]
    TAP = [abf(B_ROT + i * 512, 512) for i in range(4)]
    R = afp(FB_R, 8192)

    tok_R = None
    tok_q = tok_kv = tok_ks = None
    tok_out_prev = None
    w1slot_pe = [None, None]
    wst_pe = [None, None]
    wst_dma = [None, None]
    tok_dbc = [None, None]
    tok_dgt_free = [None, None, None, None]
    tok_ffn1_done = None
    tok_tdn = [None] * 16
    tok_tap = [None, None, None, None]
    tok_wst_pe = None
    tok_dpw_pe = None
    tok_ffn2_pe = None
    tok_ebf_ln4 = None
    q3 = q_s.rearrange("p (h t) -> p h t", t=TCORE)
    kv3 = kv_s.rearrange("p (h t) -> p h t", t=TCORE)
    ks3 = ks_s.rearrange("p (h t) -> p h t", t=TCORE)
    last_scan_dma = [(g_q[0], g_q[0].v), (g_q[1], g_q[1].v), (g_kv, g_kv.v), (g_ks, g_ks.v)]
    tok_ffn2 = None
    tok_ln4_war = []

    def emit_attn_pre(g):
        nonlocal tok_q, tok_kv, tok_ks, tok_dbc, tok_den_pe, tok_rd_pe, tok_basep
        go = g * TB
        corr_tok = {}
        qtok = []
        for i in range(4):
            w = list(last_scan_dma) + [t for t in (tok_dgt_free[i], qtok_prev[i]) if t]
            dma_gp(qsem[i], QG[:, i * 2048: i * 2048 + 2048].rearrange("p (h t) -> p h t", t=TB),
                   q3[:, 4 * i: 4 * i + 4, go:go + TB], w)
            dma_gp(qsem[i], KVG[:, i * 2048: i * 2048 + 2048].rearrange("p (h t) -> p h t", t=TB),
                   kv3[:, 4 * i: 4 * i + 4, go:go + TB], [])
            tok = dma_gp(qsem[i], KSG[:, i * 2048: i * 2048 + 2048].rearrange("p (h t) -> p h t", t=TB),
                         ks3[:, 4 * i: 4 * i + 4, go:go + TB], [])
            qtok.append(tok)
            qtok_prev[i] = tok
        tok_q = tok_kv = tok_ks = qtok[3]
        if dbg == "kvloc" and g == 0:
            for h in range(H):
                dbg_dump(KVG[:, h * TB: h * TB + TB], h * 128, 0, [tok_kv])
        if dbg == "q" and g == 0:
            for h in range(H):
                dbg_dump(QG[:, h * TB: h * TB + TB], h * 128, 0, [tok_q])
        if b_stop == "loads":
            return {}

        dbc = [abf(AB_OFF + 1024 + i * 512, 512) for i in range(2)]
        dpwtmp = afp(S_WZ, 512)
        if tok_dbc[0] is None:
            tok_dbc = [tk_sin, tk_sin]
        exp_tok = {}
        t2_tok = {}
        tok_base = v_op(lambda e, go=go: e.tensor_scalar(
            dpwtmp, iota128, float(go + 1.0), None, ALU.add),
            [tk_consts, tk_lndbc, tk_l16, tk_sin] + ([tok_basep] if tok_basep else []))

        def emit_exp(h, extra):
            exp_tok[h] = s_op(lambda e, h=h, d=dbc[h % 2]: e.activation(
                d, dpwtmp, AF.Exp, scale=lndbc[:, h:h + 1]),
                [tok_base] + extra)
        for h in range(2):
            emit_exp(h, [tok_dbc[h % 2]] if tok_dbc[h % 2] else [])
        for h in range(H):
            sl = h % 2
            v_op(lambda e, h=h, d=dbc[sl]: e.scalar_tensor_tensor(
                KVG[:, h * TB: h * TB + TB], d, s_in[:, h:h + 1],
                KVG[:, h * TB: h * TB + TB], ALU.mult, ALU.add),
                [exp_tok[h], qtok[h // 4], tk_sin])
            tok_dbc[sl] = v_op(lambda e, h=h, d=dbc[sl]: e.scalar_tensor_tensor(
                KSG[:, h * TB: h * TB + TB], d, s_in[:, h + 16:h + 17],
                KSG[:, h * TB: h * TB + TB], ALU.mult, ALU.add), [])
            corr_tok[h] = tok_dbc[sl]
            if h + 2 < H:
                emit_exp(h + 2, [tok_dbc[sl]])
        tok_basep = exp_tok[H - 1]
        tok_corr = (vch, vch.v)
        if dbg == "kvc" and g == 0:
            for h in range(H):
                dbg_dump(KVG[:, h * TB: h * TB + TB], h * 128, 0, [tok_corr])
        if b_stop == "corr":
            return {}

        t1_tok = {}
        for h in range(H):
            t1_tok[h] = g_op(lambda e, h=h, d=TDN[h]: e.tensor_mul(
                d, QG[:, h * TB: h * TB + TB], KSG[:, h * TB: h * TB + TB]),
                [corr_tok[h], qtok[h // 4]]
                + ([tok_tdn[h]] if tok_tdn[h] else [])
                + ([wst_pe[1]] if wst_pe[1] else []))
            t2_tok[h] = g_op(lambda e, h=h: e.tensor_mul(
                QG[:, h * TB: h * TB + TB], QG[:, h * TB: h * TB + TB],
                KVG[:, h * TB: h * TB + TB]), [t1_tok[h]])
        return dict(g=g, t1_tok=t1_tok, t2_tok=t2_tok)

    def emit_attn_post(st):
        nonlocal tok_den_pe, tok_rd_pe
        if st is None or "t1_tok" not in st:
            return None, None
        g = st["g"]
        t1_tok, t2_tok = st["t1_tok"], st["t2_tok"]
        den_bk = banks[2]
        for h in range(H):
            w = (den_bk.rd + [t1_tok[h], tk_consts]) if h == 0 else [t1_tok[h]]
            tok_tdn[h] = t_mm(lambda e, h=h, d=TDN[h], b=den_bk: e.matmul(
                b.t[0:16, :], eones_s[:, h * 16: h * 16 + 16], d,
                start=(h == 0), stop=(h == H - 1), skip_group_check=True),
                w, mark=True)
        den_bk.wr, den_bk.rd = (pe, pe.v), []
        tok_den_pe = den_bk.wr
        bank_read(den_bk, lambda e, b=den_bk: e.tensor_scalar_max(
            den_f, b.t[0:16, :], 1e-6), "v", [tok_rd_pe] if tok_rd_pe else [])
        v_op(lambda e: e.reciprocal(recden, den_f))
        tok_rd = v_op(lambda e: e.tensor_copy(recden_bf, recden))
        if dbg == "den" and g == 0:
            dbg_dump(den_f, 0, 0, [(vch, vch.v)])
        if b_stop == "den":
            return None, None

        for h in range(H):
            bk = banks[3 + (h % 2)]
            mm_group(bk, [lambda e, bk=bk, h=h: e.matmul(
                bk.t[:, :], sel_s[:, h * 128: h * 128 + 128], recden_bf,
                start=True, stop=True)], [tok_rd])
            bank_read(bk, lambda e, bk=bk, h=h: e.tensor_mul(
                QG[:, h * TB: h * TB + TB], QG[:, h * TB: h * TB + TB], bk.t[:, :]),
                "v", [t2_tok[h]])
        tok_o = (vch, vch.v)
        if dbg == "o" and g == 0:
            for h in range(H):
                dbg_dump(QG[:, h * TB: h * TB + TB], h * 128, 0, [tok_o])
        tok_rd_pe = (pe, pe.v)
        tok_o = (vch, vch.v)
        if dbg == "o" and g == 0:
            for h in range(H):
                dbg_dump(QG[:, h * TB: h * TB + TB], h * 128, 0, [tok_o])
        ln4w = [b.wr for b in (banks[2], banks[3], banks[4], banks[5]) if b.wr]
        return tok_o, ln4w

    tok_den_pe = None
    tok_rd_pe = None
    tok_basep = None
    qtok_prev = [None, None, None, None]
    _st0 = emit_attn_pre(0)
    attn_res = emit_attn_post(_st0) if _st0 is not None else (None, None)
    for g in range(ngB):
        go = g * TB
        if attn_res is None or attn_res[0] is None:
            break
        tok_o, prev_ln_stats = attn_res
        wsl = 0
        wpw = [t for t in (wst_pe[0], tk_wout, wst_dma[0]) if t]
        if g == 0:
            wpw += [tok_scan_all, tk_sin]
        tk_wp0 = dma_sy(
            s_wst[0],
            WST[:, 0:8192].rearrange("p (c t) -> p c t", t=512),
            wout_bf.rearrange("(c p) n -> p c n", p=128)[:, :, 0:512],
            wpw)
        wst_dma[0] = tk_wp0
        tok_R = dma_gp(s_bR, R.rearrange("p (c t) -> p c t", t=TB), x3d[:, :, go:go + TB],
                       waits=list(last_scan_dma) + [t for t in (tok_out_prev, tok_R) if t]
                       + (tkwt_all if g == 0 else []))
        ln_bank_toks = [b.wr for b in (banks[2], banks[3], banks[4], banks[5]) if b.wr]
        att_evac = {}
        for oblk in range(4):
            wsl = oblk % 2
            if oblk == 0:
                tk_wp = tk_wp0
            else:
                wpw = [t for t in (wst_pe[wsl], tk_wout, wst_dma[wsl]) if t] +                     (ln_bank_toks if oblk < 2 else [])
                if oblk == 1:
                    wpw += [tok_den_pe]
                tk_wp = dma_sy(
                    s_wst[wsl],
                    WST[:, wsl * 8192: wsl * 8192 + 8192].rearrange("p (c t) -> p c t", t=512),
                    wout_bf.rearrange("(c p) n -> p c n", p=128)[:, :, oblk * 512: oblk * 512 + 512],
                    wpw)
                wst_dma[wsl] = tk_wp
            for foc in range(4):
                fo = oblk * 4 + foc
                bk = banks[fo % 2]
                mms = [(lambda e, bk=bk, fc=fc, foc=foc, wsl=wsl: e.matmul(
                    bk.t[:, :],
                    WST[:, wsl * 8192 + fc * 512 + foc * 128: wsl * 8192 + fc * 512 + foc * 128 + 128],
                    QG[:, fc * TB: fc * TB + TB],
                    start=(fc == 0), stop=(fc == NFC - 1))) for fc in range(NFC)]
                tok_wst_pe = mm_group(bk, mms, [tk_wp, tok_o])
                wst_pe[wsl] = tok_wst_pe
                att_evac[fo] = bank_read(bk, lambda e, bk=bk, fo=fo: e.activation(
                    ATT[:, fo * TB: fo * TB + TB], bk.t[:, :], AF.Copy), "s",
                    list(prev_ln_stats) if fo < 4 else [])
        tok_att = (sch, sch.v)
        if dbg == "attn" and g == 0:
            for fc in range(NFC):
                dbg_dump(ATT[:, fc * TB: fc * TB + TB], fc * 128, 0, [tok_att])
        if b_stop == "proj":
            continue

        tok_rows = ln_make(lambda fc: ATT[:, fc * TB: fc * TB + TB], TB,
                           [], B_KVG, deps_fc=[att_evac[fc] for fc in range(NFC)])
        a_bc, b_bc, tok_bc = ln_bcast(TB, tok_rows)
        tok_r1 = None
        r1_toks = []
        for fc in range(NFC):
            sl = fc % 4
            op = g_op if fc % 2 == 0 else v_op
            ta = op(lambda e, fc=fc, d=TAP[sl], a=a_bc, b=b_bc: e.tensor_mul(
                d, ATT[:, fc * TB: fc * TB + TB], a),
                [tok_bc] + ([tok_tap[sl]] if tok_tap[sl] else []))
            tb = op(lambda e, d=TAP[sl], b=b_bc: e.tensor_add(d, d, b), [ta])
            tok_tap[sl] = op(lambda e, fc=fc, d=TAP[sl]: e.tensor_add(
                R[:, fc * TB: fc * TB + TB], R[:, fc * TB: fc * TB + TB], d),
                [tok_R, tb])
            r1_toks.append(tok_tap[sl])
            tok_r1 = tok_tap[sl]
        if dbg == "r1" and g == 0:
            for fc in range(NFC):
                dbg_dump(R[:, fc * TB: fc * TB + TB], fc * 128, 0, [tok_r1])
        if b_stop == "ln2":
            continue

        tok_rows = ln_make(lambda fc: R[:, fc * TB: fc * TB + TB], TB,
                           [], B_ATT, src_is_f32=True, deps_fc=r1_toks,
                           csl_off=B_CSL)
        a_bc, b_bc, tok_bc = ln_bcast(TB, tok_rows)
        x3_toks = []
        for fc in range(NFC):
            sl = fc % 4
            op = g_op if fc % 2 == 0 else v_op
            ta = op(lambda e, fc=fc, d=TAP[sl], a=a_bc: e.tensor_mul(
                d, R[:, fc * TB: fc * TB + TB], a),
                [tok_bc] + ([tok_tap[sl]] if tok_tap[sl] else []))
            tok_tap[sl] = op(lambda e, fc=fc, d=TAP[sl], b=b_bc: e.tensor_add(
                X3N[:, fc * TB: fc * TB + TB], d, b), [ta] + list(tok_ln4_war))
            x3_toks.append(tok_tap[sl])
        tok_x3n = x3_toks[-1]
        if dbg == "x3n" and g == 0:
            for fc in range(NFC):
                dbg_dump(X3N[:, fc * TB: fc * TB + TB], fc * 128, 0, [tok_x3n])
        if b_stop == "ln3":
            continue

        w0t3 = w0t_s.rearrange("(f p) n -> p f n", p=128)
        for nblk in range(16):
            wsl = nblk % 2
            wpw = [t for t in (wst_pe[wsl], wst_dma[wsl]) if t] + tkwt_all
            tk_wp = dma_sy(
                s_wst[wsl],
                WST[:, wsl * 8192: wsl * 8192 + 8192].rearrange("p (c t) -> p c t", t=512),
                w0t3[:, :, nblk * 512: nblk * 512 + 512],
                wpw)
            wst_dma[wsl] = tk_wp
            for ncl in range(4):
                ncg = nblk * 4 + ncl
                bk = banks[ncl % 2]
                mms = [(lambda e, bk=bk, fc=fc, ncl=ncl, wsl=wsl: e.matmul(
                    bk.t[:, :],
                    WST[:, wsl * 8192 + fc * 512 + ncl * 128: wsl * 8192 + fc * 512 + ncl * 128 + 128],
                    X3N[:, fc * TB: fc * TB + TB],
                    start=(fc == 0), stop=(fc == NFC - 1))) for fc in range(NFC)]
                tok_wst_pe = mm_group(bk, mms, [tk_wp], mm_waits=x3_toks)
                wst_pe[wsl] = tok_wst_pe
                tok_dg = bank_read(bk, lambda e, bk=bk, ncg=ncg: e.activation(
                    DGT[:, ncg * TB: ncg * TB + TB], bk.t[:, :], AF.Gelu,
                    bias=b0r[:, ncg:ncg + 1]), "s")
        tok_dg_all = (sch, sch.v)
        tok_ffn1_done = (pe, pe.v)
        if dbg == "dg" and g == 0:
            for ncg in range(8):
                dbg_dump(DGT[:, ncg * TB: ncg * TB + TB], ncg * 128, 0, [tok_dg_all])
        if b_stop == "ffn1":
            continue

        w1t3 = w1t_s.rearrange("(n p) f -> p n f", p=128)
        fh1_order = []
        for i in range(4):
            fh1_order += [2 * i, 2 * i + 1, 8 + 2 * i, 8 + 2 * i + 1,
                          16 + 2 * i, 16 + 2 * i + 1]
        fh1_order += list(range(24, 32))
        ebf_evac = {}
        for fh in range(2):
            first_w = [[t for t in (banks[j].wr,) if t] + banks[j].rd for j in range(8)]
            for j in range(8):
                banks[j].rd = []
            for it2 in range(32):
                nc2 = it2 if fh == 0 else fh1_order[it2]
                sl = it2 % 2
                w = tkwt_all + ([w1slot_pe[sl]] if w1slot_pe[sl] else [])
                if sl == 1 and it2 == 1 and fh == 0:
                    w += [tok_ffn1_done]
                tok_w1dma = dma_sy(
                    s_w1p[sl],
                    W1P2[sl].rearrange("p (n f) -> p n f", f=1024),
                    w1t3[:, nc2 * 2: nc2 * 2 + 2, fh * 1024: fh * 1024 + 1024],
                    w)
                for k in range(2):
                    ncg = nc2 * 2 + k
                    for j in range(8):
                        w2 = ([tok_w1dma] + (first_w[j] if it2 == 0 else []) +
                              ([tok_dg_all] if it2 == 0 else []))
                        t_mm(lambda e, j=j, sl=sl, k=k, ncg=ncg, it2=it2: e.matmul(
                            PSn[j][:, :], W1P2[sl][:, k * 1024 + j * 128: k * 1024 + j * 128 + 128],
                            DGT[:, ncg * TB: ncg * TB + TB],
                            start=(it2 == 0 and k == 0), stop=(it2 == 31 and k == 1),
                            skip_group_check=True),
                            w2, mark=(j == 7 and k == 1))
                w1slot_pe[sl] = (pe, pe.v)
                if fh == 1 and it2 % 6 == 5 and it2 < 24:
                    tok_dgt_free[it2 // 6] = (pe, pe.v)
            for j in range(8):
                banks[j].wr, banks[j].rd = (pe, pe.v), []
            if fh == 0:
                for j in range(8):
                    ebf_evac[j] = bank_read(banks[j], lambda e, j=j: e.activation(
                        EBF[:, j * TB: j * TB + TB], PSn[j][:, :], AF.Identity,
                        bias=b1r[:, j:j + 1]), "s")
        st_next = emit_attn_pre(g + 1) if (g + 1 < ngB) else None
        for j in range(8):
            fo = 8 + j
            ebf_evac[fo] = bank_read(banks[j], lambda e, j=j, fo=fo: e.activation(
                EBF[:, fo * TB: fo * TB + TB], PSn[j][:, :], AF.Identity,
                bias=b1r[:, fo:fo + 1]), "s")
        tok_ebf = (sch, sch.v)
        tok_ffn2 = (pe, pe.v)
        attn_res = emit_attn_post(st_next) if st_next is not None else (None, None)
        if dbg == "e" and g == 0:
            for fo in range(NFC):
                dbg_dump(EBF[:, fo * TB: fo * TB + TB], fo * 128, 0, [tok_ebf])
        if b_stop == "ffn2":
            continue

        tok_rows = ln_make(lambda fc: EBF[:, fc * TB: fc * TB + TB], TB,
                           [], B_CSL, deps_fc=[ebf_evac[fc] for fc in range(NFC)])
        a_bc, b_bc, tok_bc = ln_bcast(TB, tok_rows)
        tok_r2 = None
        for fc in range(NFC):
            sl = fc % 4
            op = g_op if fc % 2 == 0 else v_op
            ta = op(lambda e, fc=fc, d=TAP[sl], a=a_bc: e.tensor_mul(
                d, EBF[:, fc * TB: fc * TB + TB], a),
                [tok_bc] + ([tok_tap[sl]] if tok_tap[sl] else []))
            tb = op(lambda e, d=TAP[sl], b=b_bc: e.tensor_add(d, d, b), [ta])
            tok_tap[sl] = op(lambda e, fc=fc, d=TAP[sl]: e.tensor_add(
                R[:, fc * TB: fc * TB + TB], R[:, fc * TB: fc * TB + TB], d), [tb])
            tok_r2 = tok_tap[sl]
        tok_ebf_ln4 = tok_r2
        tok_ln4_war = [tok_r2] + [b.wr for b in (banks[2], banks[3]) if b.wr]

        tok_out_prev = dma_out(
            y_t.rearrange("(c p) t -> p c t", p=128)[:, :, go:go + TB],
            R.rearrange("p (c t) -> p c t", t=TB), [tok_r2])

    kb.wait("sync", (sto, sto.v))
    for s in dma_sems:
        kb.wait("sync", (s, s.v))

    all_sems = dma_sems + [vch, sch, gch, pe, ccs]

    def emit_sem_reset():
        nums = sorted(s.h.num for s in all_sems)
        nc.all_engine_barrier()
        nc.gpsimd.dma_reset(range(nums[0], nums[-1] + 1))
        for s in all_sems:
            nc.gpsimd.sem_clear(s.h)
        nc.all_engine_barrier()

    emit_sem_reset()
    with nc.Block() as block:
        @block.sync
        def _(e):
            kb.replay("sync", e)

        @block.gpsimd
        def _(e):
            kb.replay("gpsimd", e)

        @block.scalar
        def _(e):
            kb.replay("scalar", e)

        @block.vector
        def _(e):
            kb.replay("vector", e)

        @block.tensor
        def _(e):
            kb.replay("tensor", e)

    emit_sem_reset()
    es.close()
    return nc


def _host_consts():
    sel = np.zeros((H, H * 128), np.float32)
    for h in range(H):
        sel[h, h * 128:(h + 1) * 128] = 1.0
    eones = np.zeros((128, H * 16), np.float32)
    for h in range(H):
        eones[:, h * 16 + h] = 1.0
    iota = np.broadcast_to(np.arange(TB, dtype=np.float32), (16, TB)).copy()
    iota128 = np.broadcast_to(np.arange(TB, dtype=np.float32), (128, TB)).copy()
    per_core = []
    for c in range(NCORES):
        ecnt = np.zeros((128, 256), np.float32)
        mask = np.zeros((128, 256), np.float32)
        oh = np.zeros((128, 8), np.float32)
        oh[:, c] = 1.0
        b = c // 4
        for s in range(32):
            for cp in range(8):
                if (cp // 4 == b) and (cp < c):
                    ecnt[:, s * 8 + cp] = TCORE * (c - 1 - cp)
                    mask[:, s * 8 + cp] = 1.0
        per_core.append((ecnt, mask, oh))
    return sel.astype(BF_NP), eones.astype(BF_NP), iota, iota128, per_core


def kernel(L0_attn_x, L0_attn_w_qkv, L0_attn_w_out, L0_attn_decay_param,
           L0_ffn_W0, L0_ffn_bias0, L0_ffn_W1, L0_ffn_bias1,
           _dbg=None, _ngA=NGA, _ngB=NGB, _no_cc=False, _a_stop=None, _b_stop=None):
    x = np.asarray(L0_attn_x, dtype=np.float32)
    wqkv = np.ascontiguousarray(np.asarray(L0_attn_w_qkv, dtype=np.float32))
    wout = np.ascontiguousarray(np.asarray(L0_attn_w_out, dtype=np.float32))
    w0 = np.ascontiguousarray(np.asarray(L0_ffn_W0, dtype=np.float32))
    w1 = np.ascontiguousarray(np.asarray(L0_ffn_W1, dtype=np.float32))
    dec = np.ascontiguousarray(np.asarray(L0_attn_decay_param, dtype=np.float32).reshape(1, H))
    b0r = np.ascontiguousarray(np.asarray(L0_ffn_bias0, dtype=np.float32).reshape(NNC, 128).T)
    b1r = np.ascontiguousarray(np.asarray(L0_ffn_bias1, dtype=np.float32).reshape(NFC, 128).T)
    sel, eones, iota, iota128, per_core = _host_consts()

    nc = build_nc(dbg=_dbg, ngA=_ngA, ngB=_ngB, no_cc=_no_cc, a_stop=_a_stop, b_stop=_b_stop)
    in_maps = []
    for c in range(NCORES):
        b, t0 = c // 4, (c % 4) * TCORE
        xt = np.ascontiguousarray(x[b, t0:t0 + TCORE, :].T)
        ecnt, mask, oh = per_core[c]
        in_maps.append(dict(
            x_t=xt, w_qkv=wqkv, w_out=wout, w0=w0, w1=w1, d_in=dec,
            b0r_in=b0r, b1r_in=b1r, sel_in=sel, eones_in=eones, iota_in=iota,
            iota128_in=iota128, ecnt_in=ecnt, maskf_in=mask, onehot_in=oh))
    res = run_bass_kernel_spmd(nc, in_maps, core_ids=list(range(NCORES)))
    out = np.empty((2, 16384, DM), np.float32)
    for c in range(NCORES):
        b, t0 = c // 4, (c % 4) * TCORE
        out[b, t0:t0 + TCORE, :] = res.results[c]["y_t"].T
    return out
